# version 07
# baseline (speedup 1.0000x reference)
import os
import numpy as np
import ml_dtypes
import concourse.bass as bass
import concourse.mybir as mybir
from concourse.bass_utils import run_bass_kernel_spmd

F32 = mybir.dt.float32
BF16 = mybir.dt.bfloat16
AF = mybir.ActivationFunctionType
ALU = mybir.AluOpType
AXX = mybir.AxisListType.X

T = 2048
D = 2048
KV = 512
I = 8192
NH = 32
NKV = 8
NT = 16
DC = 16
ICN = 64
MAGIC = float(np.float32(1.5 * 2.0 ** 23))
NEG = -1.0e30
WN = float(2048 * 8192)
PHASE = int(os.environ.get("KPHASE", "5"))
KP2H = int(os.environ.get("KP2H", "32"))
KP2T = int(os.environ.get("KP2T", "16"))
KP2SERIAL = int(os.environ.get("KP2SERIAL", "0"))
KP2NOAV = int(os.environ.get("KP2NOAV", "0"))
KP2NOPT = int(os.environ.get("KP2NOPT", "0"))
KP2NOB = int(os.environ.get("KP2NOB", "0"))
KP2AMODE = int(os.environ.get("KP2AMODE", "0"))
KDUMP = os.environ.get("KDUMP", "")

_NC_CACHE = {}


class Walk:
    def __init__(self, nc, sems, active, engine):
        self.sems = sems
        self.active = active
        self.engine = engine
        self.c = {k: 0 for k in sems}

    def wait(self, eng, sname, val=None):
        v = self.c[sname] if val is None else val
        if eng == self.active and v > 0:
            self.engine.wait_ge(self.sems[sname], v)

    def run(self, eng, fn, inc=None, incn=1):
        if eng == self.active:
            ins = fn(self.engine)
            if inc is not None:
                ins.then_inc(self.sems[inc], incn)
        if inc is not None:
            self.c[inc] += incn


def build_nc():
    nc = bass.Bass("TRN2", target_bir_lowering=False)

    x_d = nc.dram_tensor("x", [T, D], F32, kind="ExternalInput")
    wq_d = nc.dram_tensor("wq", [D, D], F32, kind="ExternalInput")
    wk_d = nc.dram_tensor("wk", [D, KV], F32, kind="ExternalInput")
    wv_d = nc.dram_tensor("wv", [D, KV], F32, kind="ExternalInput")
    wo_d = nc.dram_tensor("wo", [D, D], F32, kind="ExternalInput")
    wg_d = nc.dram_tensor("wgT", [D, I], F32, kind="ExternalInput")
    wu_d = nc.dram_tensor("wuT", [D, I], F32, kind="ExternalInput")
    wd_d = nc.dram_tensor("wdT", [I, D], F32, kind="ExternalInput")
    cs_d = nc.dram_tensor("cs", [T, 2048], F32, kind="ExternalInput")
    mask_d = nc.dram_tensor("maskadd", [128, 128], F32, kind="ExternalInput")
    id_d = nc.dram_tensor("ident", [128, 128], BF16, kind="ExternalInput")
    y_d = nc.dram_tensor("y", [T, D], F32, kind="ExternalOutput")

    xnT_d = nc.dram_tensor("xnT_s", [128, NT * D], BF16, kind="Internal")
    xqT_d = nc.dram_tensor("xqT_s", [128, NT * D], BF16, kind="Internal")
    aT_d = nc.dram_tensor("aT_s", [128, NT * D], BF16, kind="Internal")
    h_d = nc.dram_tensor("h_s", [128, NT * I], F32, kind="Internal")
    wdq_d = nc.dram_tensor("wdq_s", [128, ICN * D], BF16, kind="Internal")
    parts_d = nc.dram_tensor("parts_s", [1, 384], F32, kind="Internal")

    A = nc.alloc_sbuf_tensor

    wbig = A("wbig", [128, 32 * 1024], BF16)
    ktdup = A("ktdup", [128, 16 * 1024], BF16)
    vt = A("vt", [128, 8 * 1024], BF16)
    pbuf = A("pbuf", [128, 8 * 1024], BF16)
    tb = A("tb", [128, 8704], BF16)
    xbuf = A("xbuf", [128, 2 * D], F32)
    f32a = A("f32a", [128, 2 * D], F32)
    csb = A("csb", [128, 2048], BF16)
    pjunk = A("pjunk", [128, 2048], BF16)
    wpre = A("wpre", [128, 2 * 1024], F32)
    atb = A("atb", [128, 2048], BF16)
    wdpc = A("wdpc", [128, 2048], BF16)

    wkv = wbig
    junk = f32a[:, 0:2048]
    xnf = f32a[:, 2048:4096]

    epsv = A("epsv", [128, 1], F32)
    onesr = A("onesr", [1, 128], F32)
    maskb = A("maskb", [128, 128], F32)
    identb = A("identb", [128, 128], BF16)
    msum = A("msum", [128, 1], F32)
    negm = A("negm", [128, 1], F32)
    ssq = A("ssq", [128, 1], F32)
    lt1 = A("lt1", [128, 1], F32)
    rstd = A("rstd", [128, 1], F32)
    negmr = A("negmr", [128, 1], F32)
    xmax = A("xmax", [128, 1], F32)
    sx_all = A("sx_all", [128, NT], F32)
    sxinv = A("sxinv", [128, 1], F32)
    mh = A("mh", [128, 1], F32)
    negb = A("negb", [128, 1], F32)
    lparts = A("lparts", [128, 4], F32)
    ljunk = A("ljunk", [128, 4], F32)
    lh = A("lh", [128, 1], F32)
    linv2 = A("linv2", [128, 2], F32)
    wsums = A("wsums", [128, 384], F32)
    wred = A("wred", [128, 3], F32)
    wrow = A("wrow", [1, 384], F32)
    wsum3 = A("wsum3", [128, 3], F32)
    wscl = A("wscl", [128, 3], F32)
    wsinv = A("wsinv", [128, 3], F32)
    sxwg = A("sxwg", [128, NT], F32)
    sxwu = A("sxwu", [128, NT], F32)
    hmaxp = A("hmaxp", [128, 256], F32)
    hmax = A("hmax", [128, 1], F32)
    shtmp = A("shtmp", [128, 1], F32)
    shinv = A("shinv", [128, 1], F32)
    shwd = A("shwd", [128, NT], F32)
    sc1 = A("sc1", [128, 1], F32)
    sc3 = A("sc3", [128, 3], F32)
    c2v = A("c2v", [128, 1], F32)
    c15v = A("c15v", [128, 1], F32)
    nt1 = A("nt1", [128, 3], F32)
    nt2 = A("nt2", [128, 3], F32)

    P = nc.alloc_psum_tensor
    pA = P("pA", [128, 512], F32)
    pB = P("pB", [128, 512], F32)
    pC = P("pC", [128, 512], F32)
    pD = P("pD", [128, 512], F32)
    pE = P("pE", [128, 1024], BF16)
    pF = P("pF", [128, 1024], BF16)
    pG = P("pG", [128, 512], F32)
    pH = P("pH", [128, 512], F32)

    semnames = ("sT", "sV", "sS", "dI", "dKV", "dCS", "dW", "dX0", "dX1",
                "dXN", "dXQ", "dP0", "dP1", "dR0", "dR1", "dAT", "dY10", "dY11",
                "dPA", "dD0", "dD1", "dH0", "dH1", "dQ0", "dQ1", "dQ2", "dQ3",
                "dO0", "dO1", "dXT")
    sems = {n: nc.alloc_semaphore(n) for n in semnames}

    def plan(active, engine):
        w = Walk(nc, sems, active, engine)
        c = w.c

        def dma(eng, out_ap, in_ap, sem):
            w.wait(eng, sem, c[sem])
            w.run(eng, lambda e: e.dma_start(out_ap, in_ap), inc=sem, incn=16)
            return c[sem]

        def nrecip(r_ap, s_ap, wdt, inc=None, nsteps=2):
            for it in range(nsteps):
                w.run("S", lambda e: e.mul(nt1[:, 0:wdt], r_ap, s_ap))
                w.run("S", lambda e: e.drain())
                w.run("S", lambda e: e.activation(nt2[:, 0:wdt], nt1[:, 0:wdt],
                                                  AF.Identity, bias=c2v[:],
                                                  scale=-1.0))
                w.run("S", lambda e: e.drain())
                last = it == nsteps - 1
                w.run("S", lambda e: e.mul(r_ap, r_ap, nt2[:, 0:wdt]),
                      inc=inc if last else None)
                if not last:
                    w.run("S", lambda e: e.drain())

        def nrsqrt(r_ap, v_ap, nsteps=2):
            for it in range(nsteps):
                w.run("S", lambda e: e.mul(nt1[:, 0:1], r_ap, r_ap))
                w.run("S", lambda e: e.drain())
                w.run("S", lambda e: e.mul(nt2[:, 0:1], nt1[:, 0:1], v_ap))
                w.run("S", lambda e: e.drain())
                w.run("S", lambda e: e.activation(nt1[:, 0:1], nt2[:, 0:1],
                                                  AF.Identity, bias=c15v[:],
                                                  scale=-0.5))
                w.run("S", lambda e: e.drain())
                w.run("S", lambda e: e.mul(r_ap, r_ap, nt1[:, 0:1]))
                w.run("S", lambda e: e.drain())

        w.run("V", lambda e: e.memset(epsv[:], 1e-5))
        w.run("V", lambda e: e.memset(onesr[:], 1.0))
        w.run("V", lambda e: e.memset(c2v[:], 2.0))
        w.run("V", lambda e: e.memset(c15v[:], 1.5))
        w.run("V", lambda e: e.memset(atb[:], 0.0))
        w.run("V", lambda e: e.memset(hmaxp[:], 0.0), inc="sV")
        w.run("S", lambda e: e.add_instruction(mybir.InstLoadActFuncSet(
            name=e.bass.get_next_instruction_name(), act_func_set_id=6,
            ins=[], outs=[])))
        dma("Y", maskb[:], mask_d.ap(), "dI")
        ident_ld = dma("Y", identb[:], id_d.ap(), "dI")
        wkv3 = wkv.ap().rearrange("p (c o) -> p c o", o=512)
        dma("G", wkv3[:, 0:16, :], wk_d.ap().rearrange("(c p) o -> p c o", p=128), "dKV")
        wkv_ld = dma("G", wkv3[:, 16:32, :],
                     wv_d.ap().rearrange("(c p) o -> p c o", p=128), "dKV")

        pre_slices = []
        for si in range(128):
            pre_slices.append((wg_d, si // 8, (si % 8) * 1024, si))
        for si in range(128):
            pre_slices.append((wu_d, si // 8, (si % 8) * 1024, 128 + si))
        for si in range(128):
            pre_slices.append((wd_d, si // 2, (si % 2) * 1024, 256 + si))
        pre = {"iss": 0, "done": 0, "ld": [], "absS": [0, 0]}

        def prepass_step(n):
            for _ in range(n):
                while pre["iss"] < 384 and pre["iss"] - pre["done"] < 2:
                    k = pre["iss"]
                    wt, rc, co, col = pre_slices[k]
                    slot = wpre[:, (k % 2) * 1024:(k % 2) * 1024 + 1024]
                    src = wt.ap().rearrange("(c p) o -> p c o", p=128)[:, rc, co:co + 1024]
                    sn = "dP%d" % (k % 2)
                    w.wait("S", sn, c[sn])
                    w.wait("S", "sS", pre["absS"][k % 2])
                    w.run("S", lambda e, s=slot, r=src: e.dma_start(s, r),
                          inc=sn, incn=16)
                    pre["ld"].append((sn, c[sn]))
                    pre["iss"] += 1
                kd = pre["done"]
                if kd >= pre["iss"]:
                    return
                wt, rc, co, col = pre_slices[kd]
                slot = wpre[:, (kd % 2) * 1024:(kd % 2) * 1024 + 1024]
                w.wait("S", pre["ld"][kd][0], pre["ld"][kd][1])
                w.run("S", lambda e, s=slot, cc=col, kp=kd % 2: e.activation(
                    pjunk[:, kp * 1024:kp * 1024 + 1024], s, AF.Abs,
                    accum_out=wsums[:, cc:cc + 1]), inc="sS")
                pre["absS"][kd % 2] = c["sS"]
                pre["done"] += 1

        xload = [0, 0]
        xfree = [0, 0]
        cs_freeV = 0
        peF = {0: 0, 1: 0}
        xnT_dma = 0
        xqT_dma = 0
        pA_freeV = 0
        pB_freeV = 0
        junk_freeV = 0
        xnTt_freeT = 0
        kb_freeT = 0

        def transpose_group(src, col0, n, bankid, dst, dstcol):
            bank = pE if bankid == 0 else pF
            w.wait("T", "sV", peF[bankid])
            for k in range(n):
                w.run("T", lambda e, kk=k: e.transpose(
                    bank[:, kk * 128:(kk + 1) * 128],
                    src[:, col0 + kk * 128:col0 + (kk + 1) * 128],
                    identb[:]), inc="sT" if k == n - 1 else None)
            w.wait("V", "sT", c["sT"])
            w.run("V", lambda e: e.tensor_copy(dst[:, dstcol:dstcol + n * 128],
                                               bank[:, 0:n * 128]), inc="sV")
            peF[bankid] = c["sV"]

        xnb = tb[:, 0:2048]
        xqb = tb[:, 2048:4096]
        xnT_t = tb[:, 4096:6144]
        xqT_t = tb[:, 6144:8192]
        kb = tb[:, 8192:8704]

        def rope6(psum_bank, cs_off, nheads, dstv):
            w.run("V", lambda e: e.drain())
            pse = psum_bank.ap().rearrange("p (h q f) -> p h q f", h=nheads, q=2)
            cE = csb[:, cs_off:cs_off + nheads * 32].rearrange("p (h f) -> p h f", f=32)
            sEa = csb[:, 1024 + cs_off:1024 + cs_off + nheads * 32].rearrange(
                "p (h f) -> p h f", f=32)
            t1 = junk[:, 0:256].rearrange("p (h f) -> p h f", f=32)[:, 0:nheads, :]
            t2 = junk[:, 256:512].rearrange("p (h f) -> p h f", f=32)[:, 0:nheads, :]
            t3 = junk[:, 512:768].rearrange("p (h f) -> p h f", f=32)[:, 0:nheads, :]
            t4 = junk[:, 768:1024].rearrange("p (h f) -> p h f", f=32)[:, 0:nheads, :]
            w.run("V", lambda e: e.tensor_mul(t1, pse[:, :, 0, :], cE))
            w.run("V", lambda e: e.tensor_mul(t2, pse[:, :, 1, :], sEa))
            w.run("V", lambda e: e.tensor_mul(t3, pse[:, :, 0, :], sEa))
            w.run("V", lambda e: e.tensor_mul(t4, pse[:, :, 1, :], cE))
            w.run("V", lambda e: e.drain())
            w.run("V", lambda e: e.tensor_sub(dstv[:, :, 0, :], t1, t2))
            w.run("V", lambda e: e.tensor_add(dstv[:, :, 1, :], t3, t4), inc="sV")

        for i in range(NT):
            sl = i % 2
            w.wait("Y", "sS", xfree[sl])
            xload[sl] = dma("Y", xbuf[:, sl * D:(sl + 1) * D],
                            x_d.ap()[i * 128:(i + 1) * 128, :], "dX%d" % sl)
            w.wait("G", "sV", cs_freeV)
            csload = dma("G", csb[:], cs_d.ap()[i * 128:(i + 1) * 128, :], "dCS")

            xb = xbuf[:, sl * D:(sl + 1) * D]
            w.wait("V", "dX%d" % sl, xload[sl])
            w.run("V", lambda e, x2=xb: e.reduce_sum(msum[:], x2, axis=AXX))
            w.run("V", lambda e: e.drain())
            w.run("V", lambda e: e.tensor_scalar(negm[:], msum[:], -1.0 / D, None,
                                                 ALU.mult), inc="sV")
            w.wait("S", "sV", c["sV"])
            w.wait("S", "sV", junk_freeV)
            w.run("S", lambda e, x2=xb: e.activation(junk[:], x2, AF.Square,
                                                     bias=negm[:], accum_out=ssq[:]))
            w.run("S", lambda e: e.drain())
            w.run("S", lambda e: e.activation(lt1[:], ssq[:], AF.Ln, bias=epsv[:],
                                              scale=1.0 / D))
            w.run("S", lambda e: e.drain())
            w.run("S", lambda e: e.activation(rstd[:], lt1[:], AF.Exp, scale=-0.5))
            w.run("S", lambda e: e.drain())
            w.run("S", lambda e: e.activation(lt1[:], ssq[:], AF.Identity,
                                              bias=epsv[:], scale=1.0 / D))
            w.run("S", lambda e: e.drain())
            nrsqrt(rstd[:], lt1[:])
            w.run("S", lambda e: e.mul(negmr[:], negm[:], rstd[:]))
            w.run("S", lambda e: e.drain())
            w.run("S", lambda e, x2=xb: e.activation(xnf[:], x2, AF.Identity,
                                                     bias=negmr[:], scale=rstd[:]))
            w.wait("S", "sT", kb_freeT)
            w.run("S", lambda e, x2=xb: e.activation(xnb, x2, AF.Identity,
                                                     bias=negmr[:], scale=rstd[:]),
                  inc="sS")
            xn_done = c["sS"]
            xfree[sl] = xn_done
            w.wait("V", "sS", xn_done)
            w.run("V", lambda e: e.reduce_max(xmax[:], xnf[:], axis=AXX,
                                              apply_absolute_value=True))
            w.run("V", lambda e: e.drain())
            w.run("V", lambda e, ii=i: e.tensor_scalar(
                sx_all[:, ii:ii + 1], xmax[:], 1.0 / 127.0, epsv[:],
                ALU.mult, ALU.add), inc="sV")
            w.wait("S", "sV", c["sV"])
            w.run("S", lambda e, ii=i: e.activation(sc1[:], sx_all[:, ii:ii + 1], AF.Ln))
            w.run("S", lambda e: e.drain())
            w.run("S", lambda e: e.activation(sxinv[:], sc1[:], AF.Exp, scale=-1.0))
            w.run("S", lambda e: e.drain())
            nrecip(sxinv[:], sx_all[:, i:i + 1], 1, inc="sS")
            w.wait("V", "sS", c["sS"])
            w.run("V", lambda e: e.tensor_scalar(junk[:], xnf[:], sxinv[:], MAGIC,
                                                 ALU.mult, ALU.add))
            w.run("V", lambda e: e.drain())
            w.run("V", lambda e: e.tensor_scalar(xqb, junk[:], MAGIC, None,
                                                 ALU.subtract), inc="sV")
            xq_done = c["sV"]

            w.wait("T", "dI", ident_ld)
            w.wait("V", "dXN", xnT_dma)
            w.wait("V", "dXQ", xqT_dma)
            w.wait("T", "sT", xnTt_freeT)
            w.wait("T", "sS", xn_done)
            transpose_group(xnb, 0, 8, 0, xnT_t, 0)
            transpose_group(xnb, 1024, 8, 1, xnT_t, 1024)
            w.wait("T", "sV", xq_done)
            transpose_group(xqb, 0, 8, 0, xqT_t, 0)
            transpose_group(xqb, 1024, 8, 1, xqT_t, 1024)
            w.wait("Y", "sV", c["sV"])
            xnT_dma = dma("Y", xnT_d.ap()[:, i * D:(i + 1) * D], xnT_t, "dXN")
            xqT_dma = dma("Y", xqT_d.ap()[:, i * D:(i + 1) * D], xqT_t, "dXQ")

            w.wait("T", "dKV", wkv_ld)
            w.wait("T", "sV", pA_freeV)
            for cc in range(DC):
                w.run("T", lambda e, k=cc: e.matmul(
                    pA[:], xnT_t[:, k * 128:(k + 1) * 128],
                    wkv[:, k * 512:(k + 1) * 512],
                    start=(k == 0), stop=(k == DC - 1)),
                    inc="sT" if cc == DC - 1 else None)
            w.wait("V", "sT", c["sT"])
            w.wait("V", "dCS", csload)
            kbv = kb.rearrange("p (h q f) -> p h q f", h=8, q=2)
            rope6(pA, 0, 8, kbv)
            pA_freeV = c["sV"]
            cs_freeV = c["sV"]
            junk_freeV = c["sV"]
            w.wait("T", "sV", c["sV"])
            w.wait("T", "sV", peF[0])
            for g in range(8):
                w.run("T", lambda e, gg=g: e.transpose(
                    pE[0:64, gg * 128:gg * 128 + 128], kb[:, gg * 64:(gg + 1) * 64],
                    identb[:]))
                w.run("T", lambda e, gg=g: e.transpose(
                    pE[64:128, gg * 128:gg * 128 + 128], kb[:, gg * 64:(gg + 1) * 64],
                    identb[:], tile_position=(0, 64)),
                    inc="sT" if g == 7 else None)
            kb_freeT = c["sT"]
            w.wait("V", "sT", c["sT"])
            ktv = ktdup.ap().rearrange("p (g t) -> p g t", t=2048)
            w.run("V", lambda e, ii=i: e.tensor_copy(
                ktv[:, :, ii * 128:(ii + 1) * 128],
                pE.ap().rearrange("p (g t) -> p g t", t=128)), inc="sV")
            peF[0] = c["sV"]

            w.wait("T", "sV", pB_freeV)
            for cc in range(DC):
                w.run("T", lambda e, k=cc: e.matmul(
                    pB[:], xnT_t[:, k * 128:(k + 1) * 128],
                    wkv[:, 8192 + k * 512:8192 + (k + 1) * 512],
                    start=(k == 0), stop=(k == DC - 1)),
                    inc="sT" if cc == DC - 1 else None)
            xnTt_freeT = c["sT"]
            w.wait("V", "sT", c["sT"])
            w.run("V", lambda e, ii=i: e.tensor_copy(vt[:, ii * 512:(ii + 1) * 512],
                                                     pB[:]), inc="sV")
            pB_freeV = c["sV"]
            prepass_step(8)

        if PHASE < 2:
            if KDUMP == "aT":
                for sn in semnames:
                    if sn.startswith("d") and sn != "dW":
                        w.wait("G", sn, c[sn])
                dma("G", y_d.ap().rearrange("(c p) o -> p c o", p=128),
                    aT_d.ap().rearrange("p (t o) -> p t o", o=2048), "dW")
            elif KDUMP == "xnT" and "2" == "2":
                for sn in semnames:
                    if sn.startswith("d") and sn != "dW":
                        w.wait("G", sn, c[sn])
                dma("G", y_d.ap().rearrange("(c p) o -> p c o", p=128),
                    xnT_d.ap().rearrange("p (t o) -> p t o", o=2048), "dW")
            for sn in semnames:
                if sn.startswith("d"):
                    w.wait("Y", sn, c[sn])
            return
        w.wait("G", "sT", c["sT"])
        wq_ld = dma("G", wbig.ap().rearrange("p (c o) -> p c o", o=2048),
                    wq_d.ap().rearrange("(c p) o -> p c o", p=128), "dW")

        xnr_ld = [0, 0]
        xnr_freeT = [c["sT"], c["sT"]]
        praw_free = [0, 0]
        pf_freeT = 0
        pt_freeT = 0
        av_freeV = [0, 0]
        mask_readV = 0
        mh_freeS = 0
        atb_dma = 0
        qbank_free = [("sV", c["sV"]), ("sV", c["sV"])]
        qb_freeT = c["sT"]
        rope_og = {0: 0, 1: 0}

        qb = tb[:, 4096:6144]
        qT_t = tb[:, 6144:8192]
        praw01 = (pbuf[:, 0:2048], pbuf[:, 2048:4096])
        pf = pbuf[:, 4096:6144]
        pt = pbuf[:, 6144:8192]

        for i in range(min(NT, KP2T)):
            sl = i % 2
            xnr = tb[:, sl * 2048:(sl + 1) * 2048]
            w.wait("Y", "sT", xnr_freeT[sl])
            xnr_ld[sl] = dma("Y", xnr, xnT_d.ap()[:, i * D:(i + 1) * D],
                             "dR%d" % sl)
            w.wait("G", "sV", cs_freeV)
            csload = dma("G", csb[:], cs_d.ap()[i * 128:(i + 1) * 128, :], "dCS")

            for og in range(4):
                bank = pA if og % 2 == 0 else pB
                w.wait("T", "dW", wq_ld)
                w.wait("T", "dR%d" % sl, xnr_ld[sl])
                kind, val = qbank_free[og % 2]
                w.wait("T", kind, val)
                w.wait("T", "sV", rope_og[og % 2])
                for cc in range(DC):
                    w.run("T", lambda e, k=cc, b2=bank, o=og, xn2=xnr: e.matmul(
                        b2[:], xn2[:, k * 128:(k + 1) * 128],
                        wbig[:, k * 2048 + o * 512:k * 2048 + (o + 1) * 512],
                        start=(k == 0), stop=(k == DC - 1)),
                        inc="sT" if cc == DC - 1 else None)
                w.wait("V", "sT", c["sT"])
                w.wait("V", "dCS", csload)
                if og == 0:
                    w.wait("V", "sT", qb_freeT)
                    w.wait("V", "sV", junk_freeV)
                    w.wait("V", "dXN", xnT_dma)
                    w.wait("V", "dXQ", xqT_dma)
                qbv = qb[:, og * 512:(og + 1) * 512].rearrange(
                    "p (h q f) -> p h q f", h=8, q=2)
                rope6(bank, og * 256, 8, qbv)
                rope_og[og % 2] = c["sV"]
            rope_done = c["sV"]
            cs_freeV = rope_done
            junk_freeV = rope_done
            qbank_free = [("sV", rope_done), ("sV", rope_done)]
            w.wait("T", "sV", rope_done)
            transpose_group(qb, 0, 8, 0, qT_t, 0)
            transpose_group(qb, 1024, 8, 1, qT_t, 1024)
            qT_ready = c["sV"]

            nj = i + 1
            chunk_bases = []
            b = nj
            while b > 0:
                lo = max(0, b - 4)
                chunk_bases.append((lo, b))
                b = lo
            slot_free = {0: ("sV", qT_ready), 1: ("sV", qT_ready)}

            def stage_a(h):
                nonlocal mask_readV, mh_freeS
                par = h % 2
                praw = praw01[par]
                w.wait("S", "sV", praw_free[par])
                for ci, (lo, hi) in enumerate(chunk_bases):
                    bank = pA if ci % 2 == 0 else pB
                    kind, val = slot_free[ci % 2]
                    w.wait("T", kind, val)
                    if ci == 0:
                        w.wait("T", "sV", mask_readV)
                    lq = qT_t[par * 64:par * 64 + 64,
                              (h // 2) * 128:(h // 2) * 128 + 128]
                    for j in range(lo, hi):
                        dest = (pG[:, 0:128] if j == i
                                else bank[:, (j - lo) * 128:(j - lo + 1) * 128])
                        w.run("T", lambda e, jj=j, d2=dest, q2=lq, p2=par, h2=h:
                              e.matmul(d2, q2,
                                       ktdup[p2 * 64:p2 * 64 + 64,
                                             (h2 // 4) * 2048 + jj * 128:
                                             (h2 // 4) * 2048 + (jj + 1) * 128],
                                       start=True, stop=True),
                              inc="sT" if j == hi - 1 else None)
                    qkc = c["sT"]
                    if KP2AMODE == 1:
                        slot_free[ci % 2] = ("sT", qkc)
                        continue
                    if ci == 0:
                        sdiag = i - lo
                        w.wait("V", "sT", qkc)
                        w.wait("V", "sS", mh_freeS)
                        w.run("V", lambda e, b2=bank, sd=sdiag: e.tensor_add(
                            b2[:, sd * 128:(sd + 1) * 128], pG[:, 0:128], maskb[:]))
                        w.run("V", lambda e: e.drain())
                        w.run("V", lambda e, b2=bank, sd=sdiag: e.reduce_max(
                            mh[:], b2[:, sd * 128:(sd + 1) * 128], axis=AXX))
                        w.run("V", lambda e: e.drain(), inc="sV")
                        mask_readV = c["sV"]
                        w.wait("S", "sV", mask_readV)
                        w.run("S", lambda e: e.mul(negb[:], mh[:], -0.125))
                        w.run("S", lambda e: e.drain())
                    else:
                        w.wait("S", "sT", qkc)
                    if KP2AMODE == 2:
                        slot_free[ci % 2] = ("sV", mask_readV)
                        continue
                    nsl = hi - lo
                    if KP2AMODE == 3:
                        w.run("S", lambda e, b2=bank, l2=lo, n2=nsl, pr=praw:
                              e.activation(pr[:, l2 * 128:(l2 + n2) * 128],
                                           b2[:, 0:n2 * 128], AF.Exp,
                                           bias=negb[:], scale=0.125), inc="sS")
                    else:
                        w.run("S", lambda e, b2=bank, l2=lo, n2=nsl, cc2=ci, pr=praw:
                              e.activation(pr[:, l2 * 128:(l2 + n2) * 128],
                                           b2[:, 0:n2 * 128], AF.Exp,
                                           bias=negb[:], scale=0.125,
                                           accum_out=lparts[:, cc2:cc2 + 1]),
                              inc="sS")
                    slot_free[ci % 2] = ("sS", c["sS"])
                    if ci == 0:
                        mh_freeS = c["sS"]
                nch = len(chunk_bases)
                if KP2AMODE == 5:
                    w.run("S", lambda e: e.drain())
                    w.run("S", lambda e, n2=nch: e.activation(
                        ljunk[:, 0:n2], lparts[:, 0:n2], AF.Copy, accum_out=lh[:]))
                    w.run("S", lambda e: e.memzero(sc1[:]), inc="sS")
                    return c["sS"]
                if KP2AMODE == 6:
                    w.run("S", lambda e: e.activation(sc1[:], epsv[:], AF.Ln))
                    w.run("S", lambda e: e.drain())
                    w.run("S", lambda e, p2=par: e.activation(
                        linv2[:, p2:p2 + 1], sc1[:], AF.Exp, scale=-1.0), inc="sS")
                    return c["sS"]
                if KP2AMODE == 4:
                    w.run("S", lambda e: e.memzero(sc1[:]), inc="sS")
                    return c["sS"]
                if KP2AMODE in (1, 2):
                    w.run("S", lambda e: e.memzero(sc1[:]), inc="sS")
                    return c["sS"]
                w.run("S", lambda e: e.drain())
                w.run("S", lambda e, n2=nch: e.activation(
                    ljunk[:, 0:n2], lparts[:, 0:n2], AF.Copy, accum_out=lh[:]))
                w.run("S", lambda e: e.drain())
                w.run("S", lambda e: e.activation(sc1[:], lh[:], AF.Ln))
                w.run("S", lambda e: e.drain())
                w.run("S", lambda e, p2=par: e.activation(
                    linv2[:, p2:p2 + 1], sc1[:], AF.Exp, scale=-1.0), inc="sS")
                return c["sS"]

            def stage_b(h, linv_done):
                nonlocal pf_freeT, pt_freeT, atb_dma
                if KP2NOB:
                    return
                par = h % 2
                praw = praw01[par]
                w.wait("V", "sS", linv_done)
                w.wait("V", "sT", pf_freeT)
                w.run("V", lambda e, p2=par, pr=praw: e.tensor_scalar(
                    pf[:, 0:nj * 128], pr[:, 0:nj * 128], linv2[:, p2:p2 + 1],
                    None, ALU.mult), inc="sV")
                scale_done = c["sV"]
                praw_free[par] = scale_done
                if KP2NOPT:
                    return
                g0 = 0
                bankid = 0
                last_ptev = 0
                while g0 < nj:
                    n = min(8, nj - g0)
                    bank = pE if bankid == 0 else pF
                    w.wait("T", "sV", scale_done)
                    w.wait("T", "sV", peF[bankid])
                    w.wait("T", "sT", pt_freeT)
                    for k in range(n):
                        w.run("T", lambda e, kk=k, g=g0, b2=bank: e.transpose(
                            b2[:, kk * 128:(kk + 1) * 128],
                            pf[:, (g + kk) * 128:(g + kk + 1) * 128], identb[:]),
                            inc="sT" if k == n - 1 else None)
                    w.wait("V", "sT", c["sT"])
                    w.run("V", lambda e, g=g0, n2=n, b2=bank: e.tensor_copy(
                        pt[:, g * 128:(g + n2) * 128], b2[:, 0:n2 * 128]), inc="sV")
                    peF[bankid] = c["sV"]
                    last_ptev = c["sV"]
                    bankid ^= 1
                    g0 += n
                pf_freeT = c["sT"]
                if KP2NOAV:
                    return
                avbank = pC if par == 0 else pD
                w.wait("T", "sV", last_ptev)
                w.wait("T", "sV", av_freeV[par])
                for j in range(nj):
                    w.run("T", lambda e, jj=j, b2=avbank, p2=par, h2=h: e.matmul(
                        b2[p2 * 64:p2 * 64 + 64, 0:128],
                        vt[:, jj * 512 + (h2 // 4) * 64:jj * 512 + (h2 // 4) * 64 + 64],
                        pt[:, jj * 128:(jj + 1) * 128],
                        start=(jj == 0), stop=(jj == nj - 1),
                        tile_position=(0, 64 * p2)),
                        inc="sT" if j == nj - 1 else None)
                pt_freeT = c["sT"]
                w.wait("V", "sT", c["sT"])
                w.wait("V", "dAT", atb_dma)
                w.run("V", lambda e, b2=avbank, p2=par, h2=h: e.tensor_copy(
                    atb[p2 * 64:p2 * 64 + 64, (h2 // 2) * 128:(h2 // 2) * 128 + 128],
                    b2[p2 * 64:p2 * 64 + 64, 0:128]), inc="sV")
                av_freeV[par] = c["sV"]

            linv_at = {}
            nheads = KP2H
            if KP2SERIAL:
                for h in range(nheads):
                    linv_at[h] = stage_a(h)
                    stage_b(h, linv_at[h])
                    if h % 8 == 0:
                        prepass_step(2)
            else:
                for h in range(nheads + 1):
                    if h < nheads:
                        linv_at[h] = stage_a(h)
                    if h >= 1:
                        stage_b(h - 1, linv_at[h - 1])
                    if h % 8 == 0:
                        prepass_step(2)
            qbank_free = [slot_free[0], slot_free[1]]
            xnr_freeT[sl] = c["sT"]
            qb_freeT = c["sT"]
            w.wait("Y", "sV", c["sV"])
            atb_dma = dma("Y", aT_d.ap()[:, i * D:(i + 1) * D], atb[:], "dAT")

        while pre["done"] < 384:
            prepass_step(4)

        if PHASE < 3:
            if KDUMP == "aT":
                for sn in semnames:
                    if sn.startswith("d") and sn != "dW":
                        w.wait("G", sn, c[sn])
                dma("G", y_d.ap().rearrange("(c p) o -> p c o", p=128),
                    aT_d.ap().rearrange("p (t o) -> p t o", o=2048), "dW")
            elif KDUMP == "xnT" and "3" == "2":
                for sn in semnames:
                    if sn.startswith("d") and sn != "dW":
                        w.wait("G", sn, c[sn])
                dma("G", y_d.ap().rearrange("(c p) o -> p c o", p=128),
                    xnT_d.ap().rearrange("p (t o) -> p t o", o=2048), "dW")
            for sn in semnames:
                if sn.startswith("d"):
                    w.wait("Y", sn, c[sn])
            return
        w.wait("G", "sT", c["sT"])
        wo_ld = dma("G", wbig.ap().rearrange("p (c o) -> p c o", o=2048),
                    wo_d.ap().rearrange("(c p) o -> p c o", p=128), "dW")

        w.wait("V", "sS", c["sS"])
        for t in range(3):
            w.run("V", lambda e, tt=t: e.reduce_sum(
                wred[:, tt:tt + 1], wsums[:, tt * 128:(tt + 1) * 128], axis=AXX))
        w.run("V", lambda e: e.drain(), inc="sV")
        w.wait("Y", "sV", c["sV"])
        pd128 = parts_d.ap().rearrange("o (p t) -> (o p) t", t=3)
        dma("Y", pd128, wred[:], "dPA")
        w.wait("Y", "dPA", c["dPA"])
        dma("Y", wrow[:], parts_d.ap(), "dPA")
        w.wait("T", "dPA", c["dPA"])
        w.wait("T", "sV", 1)
        w.run("T", lambda e: e.matmul(pH[:, 0:384], onesr[:], wrow[:],
                                      start=True, stop=True), inc="sT")
        w.wait("V", "sT", c["sT"])
        pHv = pH[:, 0:384].rearrange("p (k t) -> p k t", t=3)
        for t in range(3):
            w.run("V", lambda e, tt=t: e.reduce_sum(
                wsum3[:, tt:tt + 1], pHv[:, :, tt], axis=AXX))
        w.run("V", lambda e: e.drain())
        w.run("V", lambda e: e.tensor_scalar(wscl[:], wsum3[:], 1.0 / WN, epsv[:],
                                             ALU.mult, ALU.add), inc="sV")
        w.wait("S", "sV", c["sV"])
        w.run("S", lambda e: e.activation(sc3[:], wscl[:], AF.Ln))
        w.run("S", lambda e: e.drain())
        w.run("S", lambda e: e.activation(wsinv[:], sc3[:], AF.Exp, scale=-1.0))
        w.run("S", lambda e: e.drain())
        for t in range(3):
            nrecip(wsinv[:, t:t + 1], wscl[:, t:t + 1], 1,
                   inc="sS" if t == 2 else None)
            if t < 2:
                w.run("S", lambda e: e.drain())
        wscales_done = c["sS"]

        aT_ld = [0, 0]
        aT_freeT = [0, 0]
        y1_dma = [0, 0]
        x2load = [0, 0]
        x2_freeV = [0, 0]
        evacV = {0: 0, 1: 0}
        w.wait("Y", "sS", c["sS"])
        w.wait("Y", "sV", c["sV"])
        w.wait("Y", "dAT", c["dAT"])
        for i in range(NT):
            sl = i % 2
            aT = csb[:] if sl == 0 else atb[:]
            w.wait("Y", "sT", aT_freeT[sl])
            aT_ld[sl] = dma("Y", aT, aT_d.ap()[:, i * D:(i + 1) * D], "dR%d" % sl)
            w.wait("Y", "sV", x2_freeV[sl])
            x2load[sl] = dma("Y", xbuf[:, sl * D:(sl + 1) * D],
                             x_d.ap()[i * 128:(i + 1) * 128, :], "dX%d" % sl)
            y1 = f32a[:, sl * 2048:(sl + 1) * 2048]
            for og in range(4):
                bank = pA if og % 2 == 0 else pB
                w.wait("T", "dW", wo_ld)
                w.wait("T", "dR%d" % sl, aT_ld[sl])
                w.wait("T", "sV", evacV[og % 2])
                for cc in range(DC):
                    w.run("T", lambda e, k=cc, b2=bank, o=og, a2=aT: e.matmul(
                        b2[:], a2[:, k * 128:(k + 1) * 128],
                        wbig[:, k * 2048 + o * 512:k * 2048 + (o + 1) * 512],
                        start=(k == 0), stop=(k == DC - 1)),
                        inc="sT" if cc == DC - 1 else None)
                w.wait("V", "sT", c["sT"])
                w.wait("V", "dX%d" % sl, x2load[sl])
                w.wait("V", "dY1%d" % sl, y1_dma[sl])
                w.run("V", lambda e, b2=bank, o=og, s2=sl, yy=y1:
                      e.scalar_tensor_tensor(
                          yy[:, o * 512:(o + 1) * 512], b2[:], 1.0,
                          xbuf[:, s2 * D + o * 512:s2 * D + (o + 1) * 512],
                          ALU.mult, ALU.add), inc="sV")
                evacV[og % 2] = c["sV"]
            aT_freeT[sl] = c["sT"]
            x2_freeV[sl] = c["sV"]
            w.wait("Y", "sV", c["sV"])
            y1_dma[sl] = dma("Y", y_d.ap()[i * 128:(i + 1) * 128, :], y1,
                             "dY1%d" % sl)

        if PHASE < 4:
            if KDUMP == "aT":
                for sn in semnames:
                    if sn.startswith("d") and sn != "dW":
                        w.wait("G", sn, c[sn])
                dma("G", y_d.ap().rearrange("(c p) o -> p c o", p=128),
                    aT_d.ap().rearrange("p (t o) -> p t o", o=2048), "dW")
            elif KDUMP == "xnT" and "4" == "2":
                for sn in semnames:
                    if sn.startswith("d") and sn != "dW":
                        w.wait("G", sn, c[sn])
                dma("G", y_d.ap().rearrange("(c p) o -> p c o", p=128),
                    xnT_d.ap().rearrange("p (t o) -> p t o", o=2048), "dW")
            for sn in semnames:
                if sn.startswith("d"):
                    w.wait("Y", sn, c[sn])
            return
        w.wait("Y", "sT", c["sT"])
        xqT_ld = dma("Y", wbig[:], xqT_d.ap(), "dXT")
        w.wait("V", "sS", wscales_done)
        w.wait("V", "sV", c["sV"])
        w.wait("V", "dAT", atb_dma)
        w.wait("V", "dY10", c["dY10"])
        w.wait("V", "dY11", c["dY11"])
        for t in range(NT):
            w.run("V", lambda e, tt=t: e.tensor_scalar(
                sxwg[:, tt:tt + 1], sx_all[:, tt:tt + 1], wscl[:, 0:1], None, ALU.mult))
            w.run("V", lambda e, tt=t: e.tensor_scalar(
                sxwu[:, tt:tt + 1], sx_all[:, tt:tt + 1], wscl[:, 1:2], None, ALU.mult),
                inc="sV" if t == NT - 1 else None)
        w.run("S", lambda e: e.preload_act_table(AF.Silu))

        qslot_free = [0, 0]
        guq_lastT = [0, 0]
        wdpc_wr = [0, 0]

        def wguq_bufs(oc):
            if oc % 2 == 0:
                return ktdup[:, 0:8192], ktdup[:, 8192:16384]
            return vt[:, 0:8192], pbuf[:, 0:8192]

        qjobs = []
        for oc in range(16):
            for rc in range(DC):
                qjobs.append(("g", oc, rc))
            for rc in range(DC):
                qjobs.append(("u", oc, rc))
            for kk in range(8):
                qjobs.append(("d", oc * 8 + kk, 0))
        qs = {"k": 0}
        wg_ready = [0] * 16
        wu_ready = [0] * 16

        def quant_pump(n, cur_oc=99):
            for _ in range(n):
                if qs["k"] >= len(qjobs):
                    return
                if qs["k"] // 40 > cur_oc + 1:
                    return
                kind, a1, a2 = qjobs[qs["k"]]
                kq = qs["k"]
                qs["k"] += 1
                qsl = kq % 2
                slot = wpre[:, qsl * 1024:qsl * 1024 + 1024]
                if kind in ("g", "u"):
                    n2 = 512
                    wsel = 0 if kind == "g" else 1
                    oc2, rc = a1, a2
                    wt = wg_d if kind == "g" else wu_d
                    srcp = wt.ap().rearrange("(c p) o -> p c o", p=128)[
                        :, rc, oc2 * 512:oc2 * 512 + 512]
                    extra_waitT = guq_lastT[oc2 % 2]
                    dstg, dstu = wguq_bufs(oc2)
                    dst = dstg if kind == "g" else dstu
                    dstcol = rc * 512
                    wd_slot = None
                else:
                    idx = a1
                    ic, half = idx // 2, idx % 2
                    n2 = 1024
                    wsel = 2
                    srcp = wd_d.ap().rearrange("(c p) o -> p c o", p=128)[
                        :, ic, half * 1024:half * 1024 + 1024]
                    extra_waitT = 0
                    wd_slot = idx % 2
                    dst = wdpc
                    dstcol = wd_slot * 1024
                qa = xbuf[:, 1536:1536 + n2]
                qb2 = atb[:, 0:n2]
                w.wait("Y", "sV", qslot_free[qsl])
                qsn = "dP%d" % qsl
                ld = dma("Y", slot[:, 0:n2], srcp, qsn)
                w.wait("V", qsn, ld)
                if extra_waitT:
                    w.wait("V", "sT", extra_waitT)
                if wd_slot is not None:
                    w.wait("V", "dD%d" % wd_slot, wdpc_wr[wd_slot])
                w.run("V", lambda e, s2=slot, q2=qa, ws=wsel, n3=n2: e.tensor_scalar(
                    q2, s2[:, 0:n3], wsinv[:, ws:ws + 1], MAGIC, ALU.mult, ALU.add))
                w.run("V", lambda e: e.drain())
                w.run("V", lambda e, q2=qa, b2=qb2: e.tensor_scalar(
                    b2, q2, MAGIC, 1.0, ALU.subtract, ALU.min))
                w.run("V", lambda e: e.drain())
                w.run("V", lambda e, b2=qb2, d2=dst, dc2=dstcol, n3=n2:
                      e.tensor_scalar(d2[:, dc2:dc2 + n3], b2, -1.0, None, ALU.max),
                      inc="sV")
                qslot_free[qsl] = c["sV"]
                if kind == "g" and a2 == DC - 1:
                    wg_ready[a1] = c["sV"]
                elif kind == "u" and a2 == DC - 1:
                    wu_ready[a1] = c["sV"]
                elif kind == "d":
                    idx = a1
                    ic, half = idx // 2, idx % 2
                    w.wait("Y", "sV", c["sV"])
                    wdpc_wr[idx % 2] = dma(
                        "Y", wdq_d.ap()[:, ic * 2048 + half * 1024:
                                        ic * 2048 + half * 1024 + 1024],
                        wdpc[:, (idx % 2) * 1024:(idx % 2) * 1024 + 1024],
                        "dD%d" % (idx % 2))

        quant_pump(40, -1)

        gate_bank = [pA, pC]
        up_bank = [pB, pD]
        bfree = {"pA": 0, "pB": 0, "pC": 0, "pD": 0}
        bname = {id(pA): "pA", id(pB): "pB", id(pC): "pC", id(pD): "pD"}
        hdma = [0, 0]
        sgf_freeV = 0
        hmul_prevV = 0

        for oc in range(16):
            wgq_t, wuq_t = wguq_bufs(oc)
            for ti in range(NT):
                gb = gate_bank[ti % 2]
                ub = up_bank[ti % 2]
                w.wait("T", "dXT", xqT_ld)
                w.wait("T", "sV", wg_ready[oc])
                w.wait("T", "sV", bfree[bname[id(gb)]])
                for cc in range(DC):
                    w.run("T", lambda e, k=cc, b2=gb, t2=ti, wg2=wgq_t: e.matmul(
                        b2[:], wbig[:, t2 * 2048 + k * 128:t2 * 2048 + (k + 1) * 128],
                        wg2[:, k * 512:(k + 1) * 512],
                        start=(k == 0), stop=(k == DC - 1)),
                        inc="sT" if cc == DC - 1 else None)
                gmm = c["sT"]
                w.wait("T", "sV", wu_ready[oc])
                w.wait("T", "sV", bfree[bname[id(ub)]])
                for cc in range(DC):
                    w.run("T", lambda e, k=cc, b2=ub, t2=ti, wu2=wuq_t: e.matmul(
                        b2[:], wbig[:, t2 * 2048 + k * 128:t2 * 2048 + (k + 1) * 128],
                        wu2[:, k * 512:(k + 1) * 512],
                        start=(k == 0), stop=(k == DC - 1)),
                        inc="sT" if cc == DC - 1 else None)
                umm = c["sT"]
                guq_lastT[oc % 2] = umm
                gf = xbuf[:, 0:512]
                sgf = xbuf[:, 512:1024]
                uf = xbuf[:, 1024:1536]
                hsl = ti % 2
                hout = f32a[:, hsl * 512:hsl * 512 + 512]
                w.wait("V", "sT", gmm)
                w.wait("V", "sS", sgf_freeV)
                w.run("V", lambda e, b2=gb, t2=ti, g2=gf: e.tensor_scalar(
                    g2, b2[:], sxwg[:, t2:t2 + 1], None, ALU.mult), inc="sV")
                bfree[bname[id(gb)]] = c["sV"]
                w.wait("S", "sV", c["sV"])
                w.wait("S", "sV", hmul_prevV)
                w.run("S", lambda e, g2=gf, s2=sgf: e.activation(s2, g2, AF.Silu),
                      inc="sS")
                silu_done = c["sS"]
                sgf_freeV = silu_done
                w.wait("V", "sT", umm)
                w.run("V", lambda e, b2=ub, t2=ti, u2=uf: e.tensor_scalar(
                    u2, b2[:], sxwu[:, t2:t2 + 1], None, ALU.mult))
                w.run("V", lambda e: e.drain(), inc="sV")
                bfree[bname[id(ub)]] = c["sV"]
                w.wait("V", "sS", silu_done)
                w.wait("V", "dH%d" % hsl, hdma[hsl])
                w.run("V", lambda e, ho=hout, s2=sgf, u2=uf: e.tensor_mul(ho, s2, u2))
                w.run("V", lambda e: e.drain())
                w.run("V", lambda e, o2=oc, t2=ti, ho=hout: e.reduce_max(
                    hmaxp[:, o2 * 16 + t2:o2 * 16 + t2 + 1], ho, axis=AXX,
                    apply_absolute_value=True), inc="sV")
                hmul_prevV = c["sV"]
                w.wait("Y", "sV", c["sV"])
                hdma[hsl] = dma("Y", h_d.ap()[:, ti * I + oc * 512:
                                              ti * I + (oc + 1) * 512], hout,
                                "dH%d" % hsl)
                quant_pump(3, oc)
        quant_pump(len(qjobs))

        if PHASE < 5:
            for sn in semnames:
                if sn.startswith("d"):
                    w.wait("Y", sn, c[sn])
            return
        last_quant_V = c["sV"]
        w.run("S", lambda e: e.add_instruction(mybir.InstLoadActFuncSet(
            name=e.bass.get_next_instruction_name(), act_func_set_id=6,
            ins=[], outs=[])))
        downbank = [pA, pB, pC, pD]
        hqT_ar = [ktdup[:, 0:8192], ktdup[:, 8192:16384], vt[:, 0:8192],
                  pbuf[:, 0:8192]]
        hq = tb[:, 0:8192]
        hqslot_free = [0, 0]
        dpsum_freeV = [0, 0, 0, 0]
        out_dma = [0, 0]
        y1l_free = [0] * 4
        hq_freeT = 0
        hqT_freeT = 0
        wdsl_freeT = [0, 0]
        last_p3b_mm = c["sT"]
        last_p3b_v = c["sV"]
        rtmp = wpre[:, 0:2048]

        for tg in range(4):
            for k in range(4):
                ti = tg * 4 + k
                w.wait("V", "sS", c["sS"])
                w.wait("V", "sV", c["sV"])
                hmv = hmaxp.ap().rearrange("p (o t) -> p o t", t=16)
                w.run("V", lambda e, t2=ti: e.reduce_max(hmax[:], hmv[:, :, t2],
                                                         axis=AXX))
                w.run("V", lambda e: e.drain())
                w.run("V", lambda e: e.tensor_scalar(shtmp[:], hmax[:], 1.0 / 127.0,
                                                     epsv[:], ALU.mult, ALU.add))
                w.run("V", lambda e: e.drain())
                w.run("V", lambda e, t2=ti: e.tensor_scalar(
                    shwd[:, t2:t2 + 1], shtmp[:], wscl[:, 2:3], None, ALU.mult),
                    inc="sV")
                w.wait("S", "sV", c["sV"])
                w.run("S", lambda e: e.activation(sc1[:], shtmp[:], AF.Ln))
                w.run("S", lambda e: e.drain())
                w.run("S", lambda e: e.activation(shinv[:], sc1[:], AF.Exp,
                                                  scale=-1.0))
                w.run("S", lambda e: e.drain())
                nrecip(shinv[:], shtmp[:], 1, inc="sS")
                shinv_done = c["sS"]
                for qt in range(4):
                    qsl = qt % 2
                    hbq = f32a[:, qsl * 2048:(qsl + 1) * 2048]
                    w.wait("Y", "sV", hqslot_free[qsl])
                    if tg == 0 and k == 0 and qt < 2:
                        w.wait("Y", "sT", last_p3b_mm)
                        w.wait("Y", "sV", last_p3b_v)
                        w.wait("Y", "sS", c["sS"])
                        w.wait("Y", "dH0", c["dH0"])
                        w.wait("Y", "dH1", c["dH1"])
                    hld = dma("Y", hbq,
                              h_d.ap()[:, ti * I + qt * 2048:ti * I + (qt + 1) * 2048],
                              "dH%d" % qsl)
                    w.wait("V", "dH%d" % qsl, hld)
                    w.wait("V", "sS", shinv_done)
                    if qt == 0:
                        w.wait("V", "sT", hq_freeT)
                    w.run("V", lambda e: e.drain())
                    w.run("V", lambda e, hb2=hbq: e.tensor_scalar(
                        rtmp, hb2, shinv[:], MAGIC, ALU.mult, ALU.add))
                    w.run("V", lambda e: e.drain())
                    w.run("V", lambda e, q2=qt: e.tensor_scalar(
                        hq[:, q2 * 2048:(q2 + 1) * 2048], rtmp, MAGIC, None,
                        ALU.subtract), inc="sV")
                    hqslot_free[qsl] = c["sV"]
                rounds_done = c["sV"]
                w.wait("T", "sV", rounds_done)
                if tg > 0:
                    w.wait("T", "sT", hqT_freeT)
                for g8 in range(8):
                    transpose_group(hq, g8 * 1024, 8, g8 % 2, hqT_ar[k], g8 * 1024)
                hq_freeT = c["sT"]
            hqT_readyV = c["sV"]
            w.wait("T", "sV", hqT_readyV)
            for og in range(4):
                for k in range(4):
                    ti = tg * 4 + k
                    w.wait("Y", "sV", y1l_free[k])
                    dma("Y", xbuf[:, k * 512:(k + 1) * 512],
                        y_d.ap()[ti * 128:(ti + 1) * 128, og * 512:(og + 1) * 512],
                        "dQ%d" % k)
                y1_all = [c["dQ%d" % k] for k in range(4)]
                for ic in range(ICN):
                    ssl = ic % 2
                    w.wait("Y", "sT", wdsl_freeT[ssl])
                    if tg == 0 and og == 0 and ic < 2:
                        w.wait("Y", "sV", last_quant_V)
                        w.wait("Y", "sT", last_p3b_mm)
                    wld = dma("Y", csb[:, ssl * 512:ssl * 512 + 512],
                              wdq_d.ap()[:, ic * 2048 + og * 512:
                                         ic * 2048 + (og + 1) * 512],
                              "dD%d" % ssl)
                    w.wait("T", "dD%d" % ssl, wld)
                    if ic == 0:
                        for k in range(4):
                            w.wait("T", "sV", dpsum_freeV[k])
                    for k in range(4):
                        w.run("T", lambda e, k2=k, i2=ic, s2=ssl: e.matmul(
                            downbank[k2][:],
                            hqT_ar[k2][:, i2 * 128:(i2 + 1) * 128],
                            csb[:, s2 * 512:s2 * 512 + 512],
                            start=(i2 == 0), stop=(i2 == ICN - 1)),
                            inc="sT" if k == 3 else None)
                    wdsl_freeT[ssl] = c["sT"]
                mm_done = c["sT"]
                hqT_freeT = mm_done
                for k in range(4):
                    ti = tg * 4 + k
                    osl = k % 2
                    w.wait("V", "sT", mm_done)
                    w.wait("V", "dQ%d" % k, y1_all[k])
                    w.wait("V", "dO%d" % osl, out_dma[osl])
                    w.run("V", lambda e, k2=k, t2=ti, o3=osl:
                          e.scalar_tensor_tensor(
                              xbuf[:, 2048 + o3 * 512:2048 + (o3 + 1) * 512],
                              downbank[k2][:], shwd[:, t2:t2 + 1],
                              xbuf[:, k2 * 512:(k2 + 1) * 512],
                              ALU.mult, ALU.add), inc="sV")
                    dpsum_freeV[k] = c["sV"]
                    y1l_free[k] = c["sV"]
                    w.wait("Y", "sV", c["sV"])
                    out_dma[osl] = dma("Y", y_d.ap()[ti * 128:(ti + 1) * 128,
                                                     og * 512:(og + 1) * 512],
                                       xbuf[:, 2048 + osl * 512:
                                            2048 + (osl + 1) * 512],
                                       "dO%d" % osl)

        for sn in semnames:
            if sn.startswith("d"):
                w.wait("Y", sn, c[sn])

    with nc.Block() as block:
        @block.tensor
        def _(t):
            plan("T", t)

        @block.vector
        def _(v):
            plan("V", v)

        @block.scalar
        def _(s):
            plan("S", s)

        @block.sync
        def _(y):
            plan("Y", y)

        @block.gpsimd
        def _(g):
            plan("G", g)

    nc.finalize()
    return nc


def _host_constants():
    f = np.arange(32, dtype=np.float64)
    freq = 10000.0 ** (-f / 32.0)
    t = np.arange(T, dtype=np.float64)[:, None]
    phase = t * freq[None, :]
    cvals = np.cos(phase).astype(np.float32)
    svals = np.sin(phase).astype(np.float32)
    cs = np.empty((T, 2048), dtype=np.float32)
    cs[:, 0:1024] = np.tile(cvals, (1, 32))
    cs[:, 1024:2048] = np.tile(svals, (1, 32))
    r = np.arange(128)
    mask = np.where(r[None, :] <= r[:, None], 0.0, NEG).astype(np.float32)
    ident = np.eye(128, dtype=ml_dtypes.bfloat16)
    p64 = np.concatenate([np.arange(0, 64, 2), np.arange(1, 64, 2)])
    permq = np.concatenate([h * 64 + p64 for h in range(NH)])
    permk = np.concatenate([h * 64 + p64 for h in range(NKV)])
    return cs, mask, ident, permq, permk


def kernel(L0_attn_x, L0_attn_wq, L0_attn_wk, L0_attn_wv, L0_attn_wo,
           L0_ffn_w_gate, L0_ffn_w_up, L0_ffn_w_down):
    cs, mask, ident, permq, permk = _host_constants()
    x = np.asarray(L0_attn_x, dtype=np.float32)
    common = {
        "wq": np.ascontiguousarray(np.asarray(L0_attn_wq, np.float32)[:, permq]),
        "wk": np.ascontiguousarray(np.asarray(L0_attn_wk, np.float32)[:, permk]),
        "wv": np.ascontiguousarray(np.asarray(L0_attn_wv, np.float32)),
        "wo": np.ascontiguousarray(np.asarray(L0_attn_wo, np.float32)),
        "wgT": np.ascontiguousarray(np.asarray(L0_ffn_w_gate, np.float32).T),
        "wuT": np.ascontiguousarray(np.asarray(L0_ffn_w_up, np.float32).T),
        "wdT": np.ascontiguousarray(np.asarray(L0_ffn_w_down, np.float32).T),
        "cs": cs, "maskadd": mask, "ident": ident,
    }
    if "nc" not in _NC_CACHE:
        _NC_CACHE["nc"] = build_nc()
    nc = _NC_CACHE["nc"]
    in_maps = [dict(common, x=np.ascontiguousarray(x[c])) for c in range(8)]
    res = run_bass_kernel_spmd(nc, in_maps, core_ids=list(range(8)))
    return np.stack([np.asarray(r["y"], np.float32) for r in res.results], axis=0)


# version 08
# speedup vs baseline: 1.0850x; 1.0850x over previous
import os
import numpy as np
import ml_dtypes
import concourse.bass as bass
import concourse.mybir as mybir
from concourse.bass_utils import run_bass_kernel_spmd

F32 = mybir.dt.float32
BF16 = mybir.dt.bfloat16
I32 = mybir.dt.int32
AF = mybir.ActivationFunctionType
ALU = mybir.AluOpType
AXX = mybir.AxisListType.X

T = 2048
D = 2048
KV = 512
I = 8192
NH = 32
NKV = 8
NT = 16
DC = 16
ICN = 64
MAGIC = float(np.float32(1.5 * 2.0 ** 23))
NEG = -1.0e30
WN = float(2048 * 8192)
PHASE = int(os.environ.get("KPHASE", "5"))
KP2H = int(os.environ.get("KP2H", "32"))
KP2T = int(os.environ.get("KP2T", "16"))
KP2SERIAL = int(os.environ.get("KP2SERIAL", "0"))
KP2NOAV = int(os.environ.get("KP2NOAV", "0"))
KP2NOPT = int(os.environ.get("KP2NOPT", "0"))
KP2NOB = int(os.environ.get("KP2NOB", "0"))
KP2AMODE = int(os.environ.get("KP2AMODE", "0"))
KDUMP = os.environ.get("KDUMP", "")

_NC_CACHE = {}


class Walk:
    def __init__(self, nc, sems, active, engine):
        self.sems = sems
        self.active = active
        self.engine = engine
        self.c = {k: 0 for k in sems}

    def wait(self, eng, sname, val=None):
        v = self.c[sname] if val is None else val
        if eng == self.active and v > 0:
            self.engine.wait_ge(self.sems[sname], v)

    def run(self, eng, fn, inc=None, incn=1):
        if eng == self.active:
            ins = fn(self.engine)
            if inc is not None:
                ins.then_inc(self.sems[inc], incn)
        if inc is not None:
            self.c[inc] += incn


def build_nc():
    nc = bass.Bass("TRN2", target_bir_lowering=False)

    x_d = nc.dram_tensor("x", [T, D], F32, kind="ExternalInput")
    wq_d = nc.dram_tensor("wq", [D, D], F32, kind="ExternalInput")
    wk_d = nc.dram_tensor("wk", [D, KV], F32, kind="ExternalInput")
    wv_d = nc.dram_tensor("wv", [D, KV], F32, kind="ExternalInput")
    wo_d = nc.dram_tensor("wo", [D, D], F32, kind="ExternalInput")
    wg_d = nc.dram_tensor("wgT", [D, I], F32, kind="ExternalInput")
    wu_d = nc.dram_tensor("wuT", [D, I], F32, kind="ExternalInput")
    wd_d = nc.dram_tensor("wdT", [I, D], F32, kind="ExternalInput")
    cs_d = nc.dram_tensor("cs", [T, 2048], F32, kind="ExternalInput")
    mask_d = nc.dram_tensor("maskadd", [128, 128], F32, kind="ExternalInput")
    id_d = nc.dram_tensor("ident", [128, 128], BF16, kind="ExternalInput")
    y_d = nc.dram_tensor("y", [T, D], F32, kind="ExternalOutput")

    xnT_d = nc.dram_tensor("xnT_s", [128, NT * D], BF16, kind="Internal")
    xqT_d = nc.dram_tensor("xqT_s", [128, NT * D], BF16, kind="Internal")
    aT_d = nc.dram_tensor("aT_s", [128, NT * D], BF16, kind="Internal")
    h_d = nc.dram_tensor("h_s", [128, NT * I], F32, kind="Internal")
    wdq_d = nc.dram_tensor("wdq_s", [128, ICN * D], BF16, kind="Internal")
    parts_d = nc.dram_tensor("parts_s", [1, 384], F32, kind="Internal")

    A = nc.alloc_sbuf_tensor

    wbig = A("wbig", [128, 32 * 1024], BF16)
    ktdup = A("ktdup", [128, 16 * 1024], BF16)
    vt = A("vt", [128, 8 * 1024], BF16)
    pbuf = A("pbuf", [128, 8 * 1024], BF16)
    tb = A("tb", [128, 8704], BF16)
    xbuf = A("xbuf", [128, 2 * D], F32)
    f32a = A("f32a", [128, 2 * D], F32)
    csb = A("csb", [128, 2048], BF16)
    pjunk = A("pjunk", [128, 2048], BF16)
    wpre = A("wpre", [128, 2 * 1024], F32)
    atb = A("atb", [128, 2048], BF16)
    wdpc = A("wdpc", [128, 2048], BF16)

    wkv = wbig
    junk = f32a[:, 0:2048]
    xnf = f32a[:, 2048:4096]

    epsv = A("epsv", [128, 1], F32)
    onesr = A("onesr", [1, 128], F32)
    maskb = A("maskb", [128, 128], F32)
    identb = A("identb", [128, 128], BF16)
    msum = A("msum", [128, 1], F32)
    negm = A("negm", [128, 1], F32)
    ssq = A("ssq", [128, 1], F32)
    lt1 = A("lt1", [128, 1], F32)
    rstd = A("rstd", [128, 1], F32)
    negmr = A("negmr", [128, 1], F32)
    xmax = A("xmax", [128, 1], F32)
    sx_all = A("sx_all", [128, NT], F32)
    sxinv = A("sxinv", [128, 1], F32)
    mh = A("mh", [128, 1], F32)
    negb = A("negb", [128, 1], F32)
    lparts = A("lparts", [128, 4], F32)
    ljunk = A("ljunk", [128, 4], F32)
    lh = A("lh", [128, 1], F32)
    linv2 = A("linv2", [128, 2], F32)
    wsums = A("wsums", [128, 384], F32)
    wred = A("wred", [128, 3], F32)
    wrow = A("wrow", [1, 384], F32)
    wsum3 = A("wsum3", [128, 3], F32)
    wscl = A("wscl", [128, 3], F32)
    wsinv = A("wsinv", [128, 3], F32)
    sxwg = A("sxwg", [128, NT], F32)
    sxwu = A("sxwu", [128, NT], F32)
    hmaxp = A("hmaxp", [128, 256], F32)
    hmax = A("hmax", [128, 1], F32)
    shtmp = A("shtmp", [128, 1], F32)
    shinv = A("shinv", [128, 1], F32)
    shwd = A("shwd", [128, NT], F32)
    sc1 = A("sc1", [128, 1], F32)
    sc3 = A("sc3", [128, 3], F32)
    c2v = A("c2v", [128, 1], F32)
    c15v = A("c15v", [128, 1], F32)
    nt1 = A("nt1", [128, 3], F32)
    nt2 = A("nt2", [128, 3], F32)

    P = nc.alloc_psum_tensor
    pA = P("pA", [128, 512], F32)
    pB = P("pB", [128, 512], F32)
    pC = P("pC", [128, 512], F32)
    pD = P("pD", [128, 512], F32)
    pE = P("pE", [128, 1024], BF16)
    pF = P("pF", [128, 1024], BF16)
    pG = P("pG", [128, 512], F32)
    pH = P("pH", [128, 512], F32)

    semnames = ("sT", "sV", "sS", "dI", "dKV", "dCS", "dW", "dX0", "dX1",
                "dXN", "dXQ", "dP0", "dP1", "dR0", "dR1", "dAT", "dY10", "dY11",
                "dPA", "dD0", "dD1", "dD2", "dD3", "dH0", "dH1", "dQ0", "dQ1", "dQ2", "dQ3",
                "dO0", "dO1", "dXT")
    sems = {n: nc.alloc_semaphore(n) for n in semnames}

    def plan(active, engine):
        w = Walk(nc, sems, active, engine)
        c = w.c

        def dma(eng, out_ap, in_ap, sem):
            w.wait(eng, sem, c[sem])
            w.run(eng, lambda e: e.dma_start(out_ap, in_ap), inc=sem, incn=16)
            return c[sem]

        def nrecip(r_ap, s_ap, wdt, inc=None, nsteps=2):
            for it in range(nsteps):
                w.run("S", lambda e: e.mul(nt1[:, 0:wdt], r_ap, s_ap))
                w.run("S", lambda e: e.drain())
                w.run("S", lambda e: e.activation(nt2[:, 0:wdt], nt1[:, 0:wdt],
                                                  AF.Identity, bias=c2v[:],
                                                  scale=-1.0))
                w.run("S", lambda e: e.drain())
                last = it == nsteps - 1
                w.run("S", lambda e: e.mul(r_ap, r_ap, nt2[:, 0:wdt]),
                      inc=inc if last else None)
                if not last:
                    w.run("S", lambda e: e.drain())

        def nrsqrt(r_ap, v_ap, nsteps=2):
            for it in range(nsteps):
                w.run("S", lambda e: e.mul(nt1[:, 0:1], r_ap, r_ap))
                w.run("S", lambda e: e.drain())
                w.run("S", lambda e: e.mul(nt2[:, 0:1], nt1[:, 0:1], v_ap))
                w.run("S", lambda e: e.drain())
                w.run("S", lambda e: e.activation(nt1[:, 0:1], nt2[:, 0:1],
                                                  AF.Identity, bias=c15v[:],
                                                  scale=-0.5))
                w.run("S", lambda e: e.drain())
                w.run("S", lambda e: e.mul(r_ap, r_ap, nt1[:, 0:1]))
                w.run("S", lambda e: e.drain())

        w.run("V", lambda e: e.memset(epsv[:], 1e-5))
        w.run("V", lambda e: e.memset(onesr[:], 1.0))
        w.run("V", lambda e: e.memset(c2v[:], 2.0))
        w.run("V", lambda e: e.memset(c15v[:], 1.5))
        w.run("V", lambda e: e.memset(atb[:], 0.0))
        w.run("V", lambda e: e.memset(hmaxp[:], 0.0), inc="sV")
        w.run("S", lambda e: e.add_instruction(mybir.InstLoadActFuncSet(
            name=e.bass.get_next_instruction_name(), act_func_set_id=6,
            ins=[], outs=[])))
        dma("Y", maskb[:], mask_d.ap(), "dI")
        ident_ld = dma("Y", identb[:], id_d.ap(), "dI")
        wkv3 = wkv.ap().rearrange("p (c o) -> p c o", o=512)
        dma("G", wkv3[:, 0:16, :], wk_d.ap().rearrange("(c p) o -> p c o", p=128), "dKV")
        wkv_ld = dma("G", wkv3[:, 16:32, :],
                     wv_d.ap().rearrange("(c p) o -> p c o", p=128), "dKV")

        pre_slices = []
        for si in range(128):
            pre_slices.append((wg_d, si // 8, (si % 8) * 1024, si))
        for si in range(128):
            pre_slices.append((wu_d, si // 8, (si % 8) * 1024, 128 + si))
        for si in range(128):
            pre_slices.append((wd_d, si // 2, (si % 2) * 1024, 256 + si))
        pre = {"iss": 0, "done": 0, "ld": [], "absS": [0, 0]}

        def prepass_step(n):
            for _ in range(n):
                while pre["iss"] < 384 and pre["iss"] - pre["done"] < 2:
                    k = pre["iss"]
                    wt, rc, co, col = pre_slices[k]
                    slot = wpre[:, (k % 2) * 1024:(k % 2) * 1024 + 1024]
                    src = wt.ap().rearrange("(c p) o -> p c o", p=128)[:, rc, co:co + 1024]
                    sn = "dP%d" % (k % 2)
                    w.wait("S", sn, c[sn])
                    w.wait("S", "sS", pre["absS"][k % 2])
                    w.run("S", lambda e, s=slot, r=src: e.dma_start(s, r),
                          inc=sn, incn=16)
                    pre["ld"].append((sn, c[sn]))
                    pre["iss"] += 1
                kd = pre["done"]
                if kd >= pre["iss"]:
                    return
                wt, rc, co, col = pre_slices[kd]
                slot = wpre[:, (kd % 2) * 1024:(kd % 2) * 1024 + 1024]
                w.wait("S", pre["ld"][kd][0], pre["ld"][kd][1])
                w.run("S", lambda e, s=slot, cc=col, kp=kd % 2: e.activation(
                    pjunk[:, kp * 1024:kp * 1024 + 1024], s, AF.Abs,
                    accum_out=wsums[:, cc:cc + 1]), inc="sS")
                pre["absS"][kd % 2] = c["sS"]
                pre["done"] += 1

        xload = [0, 0]
        xfree = [0, 0]
        cs_freeV = 0
        peF = {0: 0, 1: 0}
        xnT_dma = 0
        xqT_dma = 0
        pA_freeV = 0
        pB_freeV = 0
        junk_freeV = 0
        xnTt_freeT = 0
        kb_freeT = 0

        def transpose_group(src, col0, n, bankid, dst, dstcol):
            bank = pE if bankid == 0 else pF
            w.wait("T", "sV", peF[bankid])
            for k in range(n):
                w.run("T", lambda e, kk=k: e.transpose(
                    bank[:, kk * 128:(kk + 1) * 128],
                    src[:, col0 + kk * 128:col0 + (kk + 1) * 128],
                    identb[:]), inc="sT" if k == n - 1 else None)
            w.wait("V", "sT", c["sT"])
            w.run("V", lambda e: e.tensor_copy(dst[:, dstcol:dstcol + n * 128],
                                               bank[:, 0:n * 128]), inc="sV")
            peF[bankid] = c["sV"]

        xnb = tb[:, 0:2048]
        xqb = tb[:, 2048:4096]
        xnT_t = tb[:, 4096:6144]
        xqT_t = tb[:, 6144:8192]
        kb = tb[:, 8192:8704]

        def rope6(psum_bank, cs_off, nheads, dstv):
            w.run("V", lambda e: e.drain())
            pse = psum_bank.ap().rearrange("p (h q f) -> p h q f", h=nheads, q=2)
            cE = csb[:, cs_off:cs_off + nheads * 32].rearrange("p (h f) -> p h f", f=32)
            sEa = csb[:, 1024 + cs_off:1024 + cs_off + nheads * 32].rearrange(
                "p (h f) -> p h f", f=32)
            t1 = junk[:, 0:256].rearrange("p (h f) -> p h f", f=32)[:, 0:nheads, :]
            t2 = junk[:, 256:512].rearrange("p (h f) -> p h f", f=32)[:, 0:nheads, :]
            t3 = junk[:, 512:768].rearrange("p (h f) -> p h f", f=32)[:, 0:nheads, :]
            t4 = junk[:, 768:1024].rearrange("p (h f) -> p h f", f=32)[:, 0:nheads, :]
            w.run("V", lambda e: e.tensor_mul(t1, pse[:, :, 0, :], cE))
            w.run("V", lambda e: e.tensor_mul(t2, pse[:, :, 1, :], sEa))
            w.run("V", lambda e: e.tensor_mul(t3, pse[:, :, 0, :], sEa))
            w.run("V", lambda e: e.tensor_mul(t4, pse[:, :, 1, :], cE))
            w.run("V", lambda e: e.drain())
            w.run("V", lambda e: e.tensor_sub(dstv[:, :, 0, :], t1, t2))
            w.run("V", lambda e: e.tensor_add(dstv[:, :, 1, :], t3, t4), inc="sV")

        for i in range(NT):
            sl = i % 2
            w.wait("Y", "sS", xfree[sl])
            xload[sl] = dma("Y", xbuf[:, sl * D:(sl + 1) * D],
                            x_d.ap()[i * 128:(i + 1) * 128, :], "dX%d" % sl)
            w.wait("G", "sV", cs_freeV)
            csload = dma("G", csb[:], cs_d.ap()[i * 128:(i + 1) * 128, :], "dCS")

            xb = xbuf[:, sl * D:(sl + 1) * D]
            w.wait("V", "dX%d" % sl, xload[sl])
            w.run("V", lambda e, x2=xb: e.reduce_sum(msum[:], x2, axis=AXX))
            w.run("V", lambda e: e.drain())
            w.run("V", lambda e: e.tensor_scalar(negm[:], msum[:], -1.0 / D, None,
                                                 ALU.mult), inc="sV")
            w.wait("S", "sV", c["sV"])
            w.wait("S", "sV", junk_freeV)
            w.run("S", lambda e, x2=xb: e.activation(junk[:], x2, AF.Square,
                                                     bias=negm[:], accum_out=ssq[:]))
            w.run("S", lambda e: e.drain())
            w.run("S", lambda e: e.activation(lt1[:], ssq[:], AF.Ln, bias=epsv[:],
                                              scale=1.0 / D))
            w.run("S", lambda e: e.drain())
            w.run("S", lambda e: e.activation(rstd[:], lt1[:], AF.Exp, scale=-0.5))
            w.run("S", lambda e: e.drain())
            w.run("S", lambda e: e.activation(lt1[:], ssq[:], AF.Identity,
                                              bias=epsv[:], scale=1.0 / D))
            w.run("S", lambda e: e.drain())
            nrsqrt(rstd[:], lt1[:])
            w.run("S", lambda e: e.mul(negmr[:], negm[:], rstd[:]))
            w.run("S", lambda e: e.drain())
            w.run("S", lambda e, x2=xb: e.activation(xnf[:], x2, AF.Identity,
                                                     bias=negmr[:], scale=rstd[:]))
            w.wait("S", "sT", kb_freeT)
            w.run("S", lambda e, x2=xb: e.activation(xnb, x2, AF.Identity,
                                                     bias=negmr[:], scale=rstd[:]),
                  inc="sS")
            xn_done = c["sS"]
            xfree[sl] = xn_done
            w.wait("V", "sS", xn_done)
            w.run("V", lambda e: e.reduce_max(xmax[:], xnf[:], axis=AXX,
                                              apply_absolute_value=True))
            w.run("V", lambda e: e.drain())
            w.run("V", lambda e, ii=i: e.tensor_scalar(
                sx_all[:, ii:ii + 1], xmax[:], 1.0 / 127.0, epsv[:],
                ALU.mult, ALU.add), inc="sV")
            w.wait("S", "sV", c["sV"])
            w.run("S", lambda e, ii=i: e.activation(sc1[:], sx_all[:, ii:ii + 1], AF.Ln))
            w.run("S", lambda e: e.drain())
            w.run("S", lambda e: e.activation(sxinv[:], sc1[:], AF.Exp, scale=-1.0))
            w.run("S", lambda e: e.drain())
            nrecip(sxinv[:], sx_all[:, i:i + 1], 1, inc="sS")
            w.wait("V", "sS", c["sS"])
            w.run("V", lambda e: e.tensor_scalar(junk[:], xnf[:], sxinv[:], MAGIC,
                                                 ALU.mult, ALU.add))
            w.run("V", lambda e: e.drain())
            w.run("V", lambda e: e.tensor_scalar(xqb, junk[:], MAGIC, None,
                                                 ALU.subtract), inc="sV")
            xq_done = c["sV"]

            w.wait("T", "dI", ident_ld)
            w.wait("V", "dXN", xnT_dma)
            w.wait("V", "dXQ", xqT_dma)
            w.wait("T", "sT", xnTt_freeT)
            w.wait("T", "sS", xn_done)
            transpose_group(xnb, 0, 8, 0, xnT_t, 0)
            transpose_group(xnb, 1024, 8, 1, xnT_t, 1024)
            w.wait("T", "sV", xq_done)
            transpose_group(xqb, 0, 8, 0, xqT_t, 0)
            transpose_group(xqb, 1024, 8, 1, xqT_t, 1024)
            w.wait("Y", "sV", c["sV"])
            xnT_dma = dma("Y", xnT_d.ap()[:, i * D:(i + 1) * D], xnT_t, "dXN")
            xqT_dma = dma("Y", xqT_d.ap()[:, i * D:(i + 1) * D], xqT_t, "dXQ")

            w.wait("T", "dKV", wkv_ld)
            w.wait("T", "sV", pA_freeV)
            for cc in range(DC):
                w.run("T", lambda e, k=cc: e.matmul(
                    pA[:], xnT_t[:, k * 128:(k + 1) * 128],
                    wkv[:, k * 512:(k + 1) * 512],
                    start=(k == 0), stop=(k == DC - 1)),
                    inc="sT" if cc == DC - 1 else None)
            w.wait("V", "sT", c["sT"])
            w.wait("V", "dCS", csload)
            kbv = kb.rearrange("p (h q f) -> p h q f", h=8, q=2)
            rope6(pA, 0, 8, kbv)
            pA_freeV = c["sV"]
            cs_freeV = c["sV"]
            junk_freeV = c["sV"]
            w.wait("T", "sV", c["sV"])
            w.wait("T", "sV", peF[0])
            for g in range(8):
                w.run("T", lambda e, gg=g: e.transpose(
                    pE[0:64, gg * 128:gg * 128 + 128], kb[:, gg * 64:(gg + 1) * 64],
                    identb[:]))
                w.run("T", lambda e, gg=g: e.transpose(
                    pE[64:128, gg * 128:gg * 128 + 128], kb[:, gg * 64:(gg + 1) * 64],
                    identb[:], tile_position=(0, 64)),
                    inc="sT" if g == 7 else None)
            kb_freeT = c["sT"]
            w.wait("V", "sT", c["sT"])
            ktv = ktdup.ap().rearrange("p (g t) -> p g t", t=2048)
            w.run("V", lambda e, ii=i: e.tensor_copy(
                ktv[:, :, ii * 128:(ii + 1) * 128],
                pE.ap().rearrange("p (g t) -> p g t", t=128)), inc="sV")
            peF[0] = c["sV"]

            w.wait("T", "sV", pB_freeV)
            for cc in range(DC):
                w.run("T", lambda e, k=cc: e.matmul(
                    pB[:], xnT_t[:, k * 128:(k + 1) * 128],
                    wkv[:, 8192 + k * 512:8192 + (k + 1) * 512],
                    start=(k == 0), stop=(k == DC - 1)),
                    inc="sT" if cc == DC - 1 else None)
            xnTt_freeT = c["sT"]
            w.wait("V", "sT", c["sT"])
            w.run("V", lambda e, ii=i: e.tensor_copy(vt[:, ii * 512:(ii + 1) * 512],
                                                     pB[:]), inc="sV")
            pB_freeV = c["sV"]
            prepass_step(8)

        if PHASE < 2:
            if KDUMP == "aT":
                for sn in semnames:
                    if sn.startswith("d") and sn != "dW":
                        w.wait("G", sn, c[sn])
                dma("G", y_d.ap().rearrange("(c p) o -> p c o", p=128),
                    aT_d.ap().rearrange("p (t o) -> p t o", o=2048), "dW")
            elif KDUMP == "xnT" and "2" == "2":
                for sn in semnames:
                    if sn.startswith("d") and sn != "dW":
                        w.wait("G", sn, c[sn])
                dma("G", y_d.ap().rearrange("(c p) o -> p c o", p=128),
                    xnT_d.ap().rearrange("p (t o) -> p t o", o=2048), "dW")
            for sn in semnames:
                if sn.startswith("d"):
                    w.wait("Y", sn, c[sn])
            return
        w.wait("G", "sT", c["sT"])
        wq_ld = dma("G", wbig.ap().rearrange("p (c o) -> p c o", o=2048),
                    wq_d.ap().rearrange("(c p) o -> p c o", p=128), "dW")

        xnr_ld = [0, 0]
        xnr_freeT = [c["sT"], c["sT"]]
        praw_free = [0, 0]
        pf_freeT = 0
        pt_freeT = 0
        av_freeV = [0, 0]
        mask_readV = 0
        mh_freeS = 0
        atb_dma = 0
        qbank_free = [("sV", c["sV"]), ("sV", c["sV"])]
        qb_freeT = c["sT"]
        rope_og = {0: 0, 1: 0}

        qb = tb[:, 4096:6144]
        qT_t = tb[:, 6144:8192]
        praw01 = (pbuf[:, 0:2048], pbuf[:, 2048:4096])
        pf = pbuf[:, 4096:6144]
        pt = pbuf[:, 6144:8192]

        for i in range(min(NT, KP2T)):
            sl = i % 2
            xnr = tb[:, sl * 2048:(sl + 1) * 2048]
            w.wait("Y", "sT", xnr_freeT[sl])
            xnr_ld[sl] = dma("Y", xnr, xnT_d.ap()[:, i * D:(i + 1) * D],
                             "dR%d" % sl)
            w.wait("G", "sV", cs_freeV)
            csload = dma("G", csb[:], cs_d.ap()[i * 128:(i + 1) * 128, :], "dCS")

            for og in range(4):
                bank = pA if og % 2 == 0 else pB
                w.wait("T", "dW", wq_ld)
                w.wait("T", "dR%d" % sl, xnr_ld[sl])
                kind, val = qbank_free[og % 2]
                w.wait("T", kind, val)
                w.wait("T", "sV", rope_og[og % 2])
                for cc in range(DC):
                    w.run("T", lambda e, k=cc, b2=bank, o=og, xn2=xnr: e.matmul(
                        b2[:], xn2[:, k * 128:(k + 1) * 128],
                        wbig[:, k * 2048 + o * 512:k * 2048 + (o + 1) * 512],
                        start=(k == 0), stop=(k == DC - 1)),
                        inc="sT" if cc == DC - 1 else None)
                w.wait("V", "sT", c["sT"])
                w.wait("V", "dCS", csload)
                if og == 0:
                    w.wait("V", "sT", qb_freeT)
                    w.wait("V", "sV", junk_freeV)
                    w.wait("V", "dXN", xnT_dma)
                    w.wait("V", "dXQ", xqT_dma)
                qbv = qb[:, og * 512:(og + 1) * 512].rearrange(
                    "p (h q f) -> p h q f", h=8, q=2)
                rope6(bank, og * 256, 8, qbv)
                rope_og[og % 2] = c["sV"]
            rope_done = c["sV"]
            cs_freeV = rope_done
            junk_freeV = rope_done
            qbank_free = [("sV", rope_done), ("sV", rope_done)]
            w.wait("T", "sV", rope_done)
            transpose_group(qb, 0, 8, 0, qT_t, 0)
            transpose_group(qb, 1024, 8, 1, qT_t, 1024)
            qT_ready = c["sV"]

            nj = i + 1
            chunk_bases = []
            b = nj
            while b > 0:
                lo = max(0, b - 4)
                chunk_bases.append((lo, b))
                b = lo
            slot_free = {0: ("sV", qT_ready), 1: ("sV", qT_ready)}

            def stage_a(h):
                nonlocal mask_readV, mh_freeS
                par = h % 2
                praw = praw01[par]
                w.wait("S", "sV", praw_free[par])
                for ci, (lo, hi) in enumerate(chunk_bases):
                    bank = pA if ci % 2 == 0 else pB
                    kind, val = slot_free[ci % 2]
                    w.wait("T", kind, val)
                    if ci == 0:
                        w.wait("T", "sV", mask_readV)
                    lq = qT_t[par * 64:par * 64 + 64,
                              (h // 2) * 128:(h // 2) * 128 + 128]
                    for j in range(lo, hi):
                        dest = (pG[:, 0:128] if j == i
                                else bank[:, (j - lo) * 128:(j - lo + 1) * 128])
                        w.run("T", lambda e, jj=j, d2=dest, q2=lq, p2=par, h2=h:
                              e.matmul(d2, q2,
                                       ktdup[p2 * 64:p2 * 64 + 64,
                                             (h2 // 4) * 2048 + jj * 128:
                                             (h2 // 4) * 2048 + (jj + 1) * 128],
                                       start=True, stop=True),
                              inc="sT" if j == hi - 1 else None)
                    qkc = c["sT"]
                    if KP2AMODE == 1:
                        slot_free[ci % 2] = ("sT", qkc)
                        continue
                    if ci == 0:
                        sdiag = i - lo
                        w.wait("V", "sT", qkc)
                        w.wait("V", "sS", mh_freeS)
                        w.run("V", lambda e, b2=bank, sd=sdiag: e.tensor_add(
                            b2[:, sd * 128:(sd + 1) * 128], pG[:, 0:128], maskb[:]))
                        w.run("V", lambda e: e.drain())
                        w.run("V", lambda e, b2=bank, sd=sdiag: e.reduce_max(
                            mh[:], b2[:, sd * 128:(sd + 1) * 128], axis=AXX))
                        w.run("V", lambda e: e.drain(), inc="sV")
                        mask_readV = c["sV"]
                        w.wait("S", "sV", mask_readV)
                        w.run("S", lambda e: e.mul(negb[:], mh[:], -0.125))
                        w.run("S", lambda e: e.drain())
                    else:
                        w.wait("S", "sT", qkc)
                    if KP2AMODE == 2:
                        slot_free[ci % 2] = ("sV", mask_readV)
                        continue
                    nsl = hi - lo
                    if KP2AMODE == 3:
                        w.run("S", lambda e, b2=bank, l2=lo, n2=nsl, pr=praw:
                              e.activation(pr[:, l2 * 128:(l2 + n2) * 128],
                                           b2[:, 0:n2 * 128], AF.Exp,
                                           bias=negb[:], scale=0.125), inc="sS")
                    else:
                        w.run("S", lambda e, b2=bank, l2=lo, n2=nsl, cc2=ci, pr=praw:
                              e.activation(pr[:, l2 * 128:(l2 + n2) * 128],
                                           b2[:, 0:n2 * 128], AF.Exp,
                                           bias=negb[:], scale=0.125,
                                           accum_out=lparts[:, cc2:cc2 + 1]),
                              inc="sS")
                    slot_free[ci % 2] = ("sS", c["sS"])
                    if ci == 0:
                        mh_freeS = c["sS"]
                nch = len(chunk_bases)
                if KP2AMODE == 5:
                    w.run("S", lambda e: e.drain())
                    w.run("S", lambda e, n2=nch: e.activation(
                        ljunk[:, 0:n2], lparts[:, 0:n2], AF.Copy, accum_out=lh[:]))
                    w.run("S", lambda e: e.memzero(sc1[:]), inc="sS")
                    return c["sS"]
                if KP2AMODE == 6:
                    w.run("S", lambda e: e.activation(sc1[:], epsv[:], AF.Ln))
                    w.run("S", lambda e: e.drain())
                    w.run("S", lambda e, p2=par: e.activation(
                        linv2[:, p2:p2 + 1], sc1[:], AF.Exp, scale=-1.0), inc="sS")
                    return c["sS"]
                if KP2AMODE == 4:
                    w.run("S", lambda e: e.memzero(sc1[:]), inc="sS")
                    return c["sS"]
                if KP2AMODE in (1, 2):
                    w.run("S", lambda e: e.memzero(sc1[:]), inc="sS")
                    return c["sS"]
                w.run("S", lambda e: e.drain())
                w.run("S", lambda e, n2=nch: e.activation(
                    ljunk[:, 0:n2], lparts[:, 0:n2], AF.Copy, accum_out=lh[:]))
                w.run("S", lambda e: e.drain())
                w.run("S", lambda e: e.activation(sc1[:], lh[:], AF.Ln))
                w.run("S", lambda e: e.drain())
                w.run("S", lambda e, p2=par: e.activation(
                    linv2[:, p2:p2 + 1], sc1[:], AF.Exp, scale=-1.0), inc="sS")
                return c["sS"]

            def stage_b(h, linv_done):
                nonlocal pf_freeT, pt_freeT, atb_dma
                if KP2NOB:
                    return
                par = h % 2
                praw = praw01[par]
                w.wait("V", "sS", linv_done)
                w.wait("V", "sT", pf_freeT)
                w.run("V", lambda e, p2=par, pr=praw: e.tensor_scalar(
                    pf[:, 0:nj * 128], pr[:, 0:nj * 128], linv2[:, p2:p2 + 1],
                    None, ALU.mult), inc="sV")
                scale_done = c["sV"]
                praw_free[par] = scale_done
                if KP2NOPT:
                    return
                g0 = 0
                bankid = 0
                last_ptev = 0
                while g0 < nj:
                    n = min(8, nj - g0)
                    bank = pE if bankid == 0 else pF
                    w.wait("T", "sV", scale_done)
                    w.wait("T", "sV", peF[bankid])
                    w.wait("T", "sT", pt_freeT)
                    for k in range(n):
                        w.run("T", lambda e, kk=k, g=g0, b2=bank: e.transpose(
                            b2[:, kk * 128:(kk + 1) * 128],
                            pf[:, (g + kk) * 128:(g + kk + 1) * 128], identb[:]),
                            inc="sT" if k == n - 1 else None)
                    w.wait("V", "sT", c["sT"])
                    w.run("V", lambda e, g=g0, n2=n, b2=bank: e.tensor_copy(
                        pt[:, g * 128:(g + n2) * 128], b2[:, 0:n2 * 128]), inc="sV")
                    peF[bankid] = c["sV"]
                    last_ptev = c["sV"]
                    bankid ^= 1
                    g0 += n
                pf_freeT = c["sT"]
                if KP2NOAV:
                    return
                avbank = pC if par == 0 else pD
                w.wait("T", "sV", last_ptev)
                w.wait("T", "sV", av_freeV[par])
                for j in range(nj):
                    w.run("T", lambda e, jj=j, b2=avbank, p2=par, h2=h: e.matmul(
                        b2[p2 * 64:p2 * 64 + 64, 0:128],
                        vt[:, jj * 512 + (h2 // 4) * 64:jj * 512 + (h2 // 4) * 64 + 64],
                        pt[:, jj * 128:(jj + 1) * 128],
                        start=(jj == 0), stop=(jj == nj - 1),
                        tile_position=(0, 64 * p2)),
                        inc="sT" if j == nj - 1 else None)
                pt_freeT = c["sT"]
                w.wait("V", "sT", c["sT"])
                w.wait("V", "dAT", atb_dma)
                w.run("V", lambda e, b2=avbank, p2=par, h2=h: e.tensor_copy(
                    atb[p2 * 64:p2 * 64 + 64, (h2 // 2) * 128:(h2 // 2) * 128 + 128],
                    b2[p2 * 64:p2 * 64 + 64, 0:128]), inc="sV")
                av_freeV[par] = c["sV"]

            linv_at = {}
            nheads = KP2H
            if KP2SERIAL:
                for h in range(nheads):
                    linv_at[h] = stage_a(h)
                    stage_b(h, linv_at[h])
                    if h % 8 == 0:
                        prepass_step(2)
            else:
                for h in range(nheads + 1):
                    if h < nheads:
                        linv_at[h] = stage_a(h)
                    if h >= 1:
                        stage_b(h - 1, linv_at[h - 1])
                    if h % 8 == 0:
                        prepass_step(2)
            qbank_free = [slot_free[0], slot_free[1]]
            xnr_freeT[sl] = c["sT"]
            qb_freeT = c["sT"]
            w.wait("Y", "sV", c["sV"])
            atb_dma = dma("Y", aT_d.ap()[:, i * D:(i + 1) * D], atb[:], "dAT")

        while pre["done"] < 384:
            prepass_step(4)

        if PHASE < 3:
            if KDUMP == "aT":
                for sn in semnames:
                    if sn.startswith("d") and sn != "dW":
                        w.wait("G", sn, c[sn])
                dma("G", y_d.ap().rearrange("(c p) o -> p c o", p=128),
                    aT_d.ap().rearrange("p (t o) -> p t o", o=2048), "dW")
            elif KDUMP == "xnT" and "3" == "2":
                for sn in semnames:
                    if sn.startswith("d") and sn != "dW":
                        w.wait("G", sn, c[sn])
                dma("G", y_d.ap().rearrange("(c p) o -> p c o", p=128),
                    xnT_d.ap().rearrange("p (t o) -> p t o", o=2048), "dW")
            for sn in semnames:
                if sn.startswith("d"):
                    w.wait("Y", sn, c[sn])
            return
        w.wait("G", "sT", c["sT"])
        wo_ld = dma("G", wbig.ap().rearrange("p (c o) -> p c o", o=2048),
                    wo_d.ap().rearrange("(c p) o -> p c o", p=128), "dW")

        w.wait("V", "sS", c["sS"])
        for t in range(3):
            w.run("V", lambda e, tt=t: e.reduce_sum(
                wred[:, tt:tt + 1], wsums[:, tt * 128:(tt + 1) * 128], axis=AXX))
        w.run("V", lambda e: e.drain(), inc="sV")
        w.wait("Y", "sV", c["sV"])
        pd128 = parts_d.ap().rearrange("o (p t) -> (o p) t", t=3)
        dma("Y", pd128, wred[:], "dPA")
        w.wait("Y", "dPA", c["dPA"])
        dma("Y", wrow[:], parts_d.ap(), "dPA")
        w.wait("T", "dPA", c["dPA"])
        w.wait("T", "sV", 1)
        w.run("T", lambda e: e.matmul(pH[:, 0:384], onesr[:], wrow[:],
                                      start=True, stop=True), inc="sT")
        w.wait("V", "sT", c["sT"])
        pHv = pH[:, 0:384].rearrange("p (k t) -> p k t", t=3)
        for t in range(3):
            w.run("V", lambda e, tt=t: e.reduce_sum(
                wsum3[:, tt:tt + 1], pHv[:, :, tt], axis=AXX))
        w.run("V", lambda e: e.drain())
        w.run("V", lambda e: e.tensor_scalar(wscl[:], wsum3[:], 1.0 / WN, epsv[:],
                                             ALU.mult, ALU.add), inc="sV")
        w.wait("S", "sV", c["sV"])
        w.run("S", lambda e: e.activation(sc3[:], wscl[:], AF.Ln))
        w.run("S", lambda e: e.drain())
        w.run("S", lambda e: e.activation(wsinv[:], sc3[:], AF.Exp, scale=-1.0))
        w.run("S", lambda e: e.drain())
        for t in range(3):
            nrecip(wsinv[:, t:t + 1], wscl[:, t:t + 1], 1,
                   inc="sS" if t == 2 else None)
            if t < 2:
                w.run("S", lambda e: e.drain())
        wscales_done = c["sS"]

        aT_ld = [0, 0]
        aT_freeT = [0, 0]
        y1_dma = [0, 0]
        x2load = [0, 0]
        x2_freeV = [0, 0]
        evacV = {0: 0, 1: 0}
        w.wait("Y", "sS", c["sS"])
        w.wait("Y", "sV", c["sV"])
        w.wait("Y", "dAT", c["dAT"])
        for i in range(NT):
            sl = i % 2
            aT = csb[:] if sl == 0 else atb[:]
            w.wait("Y", "sT", aT_freeT[sl])
            aT_ld[sl] = dma("Y", aT, aT_d.ap()[:, i * D:(i + 1) * D], "dR%d" % sl)
            w.wait("Y", "sV", x2_freeV[sl])
            x2load[sl] = dma("Y", xbuf[:, sl * D:(sl + 1) * D],
                             x_d.ap()[i * 128:(i + 1) * 128, :], "dX%d" % sl)
            y1 = f32a[:, sl * 2048:(sl + 1) * 2048]
            for og in range(4):
                bank = pA if og % 2 == 0 else pB
                w.wait("T", "dW", wo_ld)
                w.wait("T", "dR%d" % sl, aT_ld[sl])
                w.wait("T", "sV", evacV[og % 2])
                for cc in range(DC):
                    w.run("T", lambda e, k=cc, b2=bank, o=og, a2=aT: e.matmul(
                        b2[:], a2[:, k * 128:(k + 1) * 128],
                        wbig[:, k * 2048 + o * 512:k * 2048 + (o + 1) * 512],
                        start=(k == 0), stop=(k == DC - 1)),
                        inc="sT" if cc == DC - 1 else None)
                w.wait("V", "sT", c["sT"])
                w.wait("V", "dX%d" % sl, x2load[sl])
                w.wait("V", "dY1%d" % sl, y1_dma[sl])
                w.run("V", lambda e, b2=bank, o=og, s2=sl, yy=y1:
                      e.scalar_tensor_tensor(
                          yy[:, o * 512:(o + 1) * 512], b2[:], 1.0,
                          xbuf[:, s2 * D + o * 512:s2 * D + (o + 1) * 512],
                          ALU.mult, ALU.add), inc="sV")
                evacV[og % 2] = c["sV"]
            aT_freeT[sl] = c["sT"]
            x2_freeV[sl] = c["sV"]
            w.wait("Y", "sV", c["sV"])
            y1_dma[sl] = dma("Y", y_d.ap()[i * 128:(i + 1) * 128, :], y1,
                             "dY1%d" % sl)

        if PHASE < 4:
            if KDUMP == "aT":
                for sn in semnames:
                    if sn.startswith("d") and sn != "dW":
                        w.wait("G", sn, c[sn])
                dma("G", y_d.ap().rearrange("(c p) o -> p c o", p=128),
                    aT_d.ap().rearrange("p (t o) -> p t o", o=2048), "dW")
            elif KDUMP == "xnT" and "4" == "2":
                for sn in semnames:
                    if sn.startswith("d") and sn != "dW":
                        w.wait("G", sn, c[sn])
                dma("G", y_d.ap().rearrange("(c p) o -> p c o", p=128),
                    xnT_d.ap().rearrange("p (t o) -> p t o", o=2048), "dW")
            for sn in semnames:
                if sn.startswith("d"):
                    w.wait("Y", sn, c[sn])
            return
        w.wait("Y", "sT", c["sT"])
        xqT_ld = dma("Y", wbig[:], xqT_d.ap(), "dXT")
        w.wait("V", "sS", wscales_done)
        w.wait("V", "sV", c["sV"])
        w.wait("V", "dAT", atb_dma)
        w.wait("V", "dY10", c["dY10"])
        w.wait("V", "dY11", c["dY11"])
        for t in range(NT):
            w.run("V", lambda e, tt=t: e.tensor_scalar(
                sxwg[:, tt:tt + 1], sx_all[:, tt:tt + 1], wscl[:, 0:1], None, ALU.mult))
            w.run("V", lambda e, tt=t: e.tensor_scalar(
                sxwu[:, tt:tt + 1], sx_all[:, tt:tt + 1], wscl[:, 1:2], None, ALU.mult),
                inc="sV" if t == NT - 1 else None)
        w.run("S", lambda e: e.preload_act_table(AF.Silu))

        qslot_free = [0, 0]
        guq_lastT = [0, 0]
        wdpc_wr = [0, 0]

        def wguq_bufs(oc):
            if oc % 2 == 0:
                return ktdup[:, 0:8192], ktdup[:, 8192:16384]
            return vt[:, 0:8192], pbuf[:, 0:8192]

        qjobs = []
        for oc in range(16):
            for rc in range(DC):
                qjobs.append(("g", oc, rc))
            for rc in range(DC):
                qjobs.append(("u", oc, rc))
            for kk in range(8):
                qjobs.append(("d", oc * 8 + kk, 0))
        qs = {"k": 0}
        wg_ready = [0] * 16
        wu_ready = [0] * 16

        def quant_pump(n, cur_oc=99):
            for _ in range(n):
                if qs["k"] >= len(qjobs):
                    return
                if qs["k"] // 40 > cur_oc + 1:
                    return
                kind, a1, a2 = qjobs[qs["k"]]
                kq = qs["k"]
                qs["k"] += 1
                qsl = kq % 2
                slot = wpre[:, qsl * 1024:qsl * 1024 + 1024]
                if kind in ("g", "u"):
                    n2 = 512
                    wsel = 0 if kind == "g" else 1
                    oc2, rc = a1, a2
                    wt = wg_d if kind == "g" else wu_d
                    srcp = wt.ap().rearrange("(c p) o -> p c o", p=128)[
                        :, rc, oc2 * 512:oc2 * 512 + 512]
                    extra_waitT = guq_lastT[oc2 % 2]
                    dstg, dstu = wguq_bufs(oc2)
                    dst = dstg if kind == "g" else dstu
                    dstcol = rc * 512
                    wd_slot = None
                else:
                    idx = a1
                    ic, half = idx // 2, idx % 2
                    n2 = 1024
                    wsel = 2
                    srcp = wd_d.ap().rearrange("(c p) o -> p c o", p=128)[
                        :, ic, half * 1024:half * 1024 + 1024]
                    extra_waitT = 0
                    wd_slot = idx % 2
                    dst = wdpc
                    dstcol = wd_slot * 1024
                qa = xbuf[:, 1536:1536 + n2]
                qb2 = atb[:, 0:n2]
                w.wait("Y", "sV", qslot_free[qsl])
                qsn = "dP%d" % qsl
                ld = dma("Y", slot[:, 0:n2], srcp, qsn)
                w.wait("V", qsn, ld)
                if extra_waitT:
                    w.wait("V", "sT", extra_waitT)
                if wd_slot is not None:
                    w.wait("V", "dD%d" % wd_slot, wdpc_wr[wd_slot])
                w.run("V", lambda e, s2=slot, q2=qa, ws=wsel, n3=n2: e.tensor_scalar(
                    q2, s2[:, 0:n3], wsinv[:, ws:ws + 1], MAGIC, ALU.mult, ALU.add))
                w.run("V", lambda e: e.drain())
                w.run("V", lambda e, q2=qa, b2=qb2: e.tensor_scalar(
                    b2, q2, MAGIC, 1.0, ALU.subtract, ALU.min))
                w.run("V", lambda e: e.drain())
                w.run("V", lambda e, b2=qb2, d2=dst, dc2=dstcol, n3=n2:
                      e.tensor_scalar(d2[:, dc2:dc2 + n3], b2, -1.0, None, ALU.max),
                      inc="sV")
                qslot_free[qsl] = c["sV"]
                if kind == "g" and a2 == DC - 1:
                    wg_ready[a1] = c["sV"]
                elif kind == "u" and a2 == DC - 1:
                    wu_ready[a1] = c["sV"]
                elif kind == "d":
                    idx = a1
                    ic, half = idx // 2, idx % 2
                    w.wait("Y", "sV", c["sV"])
                    wdpc_wr[idx % 2] = dma(
                        "Y", wdq_d.ap()[:, ic * 2048 + half * 1024:
                                        ic * 2048 + half * 1024 + 1024],
                        wdpc[:, (idx % 2) * 1024:(idx % 2) * 1024 + 1024],
                        "dD%d" % (idx % 2))

        quant_pump(40, -1)

        gate_bank = [pA, pC]
        up_bank = [pB, pD]
        bfree = {"pA": 0, "pB": 0, "pC": 0, "pD": 0}
        bname = {id(pA): "pA", id(pB): "pB", id(pC): "pC", id(pD): "pD"}
        hdma = [0, 0]
        sgf_freeV = 0
        hmul_prevV = 0

        for oc in range(16):
            wgq_t, wuq_t = wguq_bufs(oc)
            for ti in range(NT):
                gb = gate_bank[ti % 2]
                ub = up_bank[ti % 2]
                w.wait("T", "dXT", xqT_ld)
                w.wait("T", "sV", wg_ready[oc])
                w.wait("T", "sV", bfree[bname[id(gb)]])
                for cc in range(DC):
                    w.run("T", lambda e, k=cc, b2=gb, t2=ti, wg2=wgq_t: e.matmul(
                        b2[:], wbig[:, t2 * 2048 + k * 128:t2 * 2048 + (k + 1) * 128],
                        wg2[:, k * 512:(k + 1) * 512],
                        start=(k == 0), stop=(k == DC - 1)),
                        inc="sT" if cc == DC - 1 else None)
                gmm = c["sT"]
                w.wait("T", "sV", wu_ready[oc])
                w.wait("T", "sV", bfree[bname[id(ub)]])
                for cc in range(DC):
                    w.run("T", lambda e, k=cc, b2=ub, t2=ti, wu2=wuq_t: e.matmul(
                        b2[:], wbig[:, t2 * 2048 + k * 128:t2 * 2048 + (k + 1) * 128],
                        wu2[:, k * 512:(k + 1) * 512],
                        start=(k == 0), stop=(k == DC - 1)),
                        inc="sT" if cc == DC - 1 else None)
                umm = c["sT"]
                guq_lastT[oc % 2] = umm
                gf = xbuf[:, 0:512]
                sgf = xbuf[:, 512:1024]
                uf = xbuf[:, 1024:1536]
                hsl = ti % 2
                hout = f32a[:, hsl * 512:hsl * 512 + 512]
                w.wait("V", "sT", gmm)
                w.wait("V", "sS", sgf_freeV)
                w.run("V", lambda e, b2=gb, t2=ti, g2=gf: e.tensor_scalar(
                    g2, b2[:], sxwg[:, t2:t2 + 1], None, ALU.mult), inc="sV")
                bfree[bname[id(gb)]] = c["sV"]
                w.wait("S", "sV", c["sV"])
                w.wait("S", "sV", hmul_prevV)
                w.run("S", lambda e, g2=gf, s2=sgf: e.activation(s2, g2, AF.Silu),
                      inc="sS")
                silu_done = c["sS"]
                sgf_freeV = silu_done
                w.wait("V", "sT", umm)
                w.run("V", lambda e, b2=ub, t2=ti, u2=uf: e.tensor_scalar(
                    u2, b2[:], sxwu[:, t2:t2 + 1], None, ALU.mult))
                w.run("V", lambda e: e.drain(), inc="sV")
                bfree[bname[id(ub)]] = c["sV"]
                w.wait("V", "sS", silu_done)
                w.wait("V", "dH%d" % hsl, hdma[hsl])
                w.run("V", lambda e, ho=hout, s2=sgf, u2=uf: e.tensor_mul(ho, s2, u2))
                w.run("V", lambda e: e.drain())
                w.run("V", lambda e, o2=oc, t2=ti, ho=hout: e.reduce_max(
                    hmaxp[:, o2 * 16 + t2:o2 * 16 + t2 + 1], ho, axis=AXX,
                    apply_absolute_value=True), inc="sV")
                hmul_prevV = c["sV"]
                w.wait("Y", "sV", c["sV"])
                hdma[hsl] = dma("Y", h_d.ap()[:, ti * I + oc * 512:
                                              ti * I + (oc + 1) * 512], hout,
                                "dH%d" % hsl)
                quant_pump(3, oc)
        quant_pump(len(qjobs))

        if PHASE < 5:
            for sn in semnames:
                if sn.startswith("d"):
                    w.wait("Y", sn, c[sn])
            return
        last_quant_V = c["sV"]
        w.run("S", lambda e: e.add_instruction(mybir.InstLoadActFuncSet(
            name=e.bass.get_next_instruction_name(), act_func_set_id=6,
            ins=[], outs=[])))
        downbank = [pA, pB, pC, pD]
        hqT_ar = [ktdup[:, 0:8192], ktdup[:, 8192:16384], vt[:, 0:8192],
                  pbuf[:, 0:8192]]
        hq = tb[:, 0:8192]
        hqslot_free = [0, 0]
        dpsum_freeV = [0, 0, 0, 0]
        out_dma = [0, 0]
        y1l_free = [0] * 4
        hq_freeT = 0
        hqT_freeT = 0
        wdsl_freeT = [0, 0, 0, 0]
        wd_lds = [0, 0, 0, 0]
        last_p3b_mm = c["sT"]
        last_p3b_v = c["sV"]
        rtmp = wpre[:, 0:2048]

        for tg in range(4):
            for k in range(4):
                ti = tg * 4 + k
                w.wait("V", "sS", c["sS"])
                w.wait("V", "sV", c["sV"])
                hmv = hmaxp.ap().rearrange("p (o t) -> p o t", t=16)
                w.run("V", lambda e, t2=ti: e.reduce_max(hmax[:], hmv[:, :, t2],
                                                         axis=AXX))
                w.run("V", lambda e: e.drain())
                w.run("V", lambda e: e.tensor_scalar(shtmp[:], hmax[:], 1.0 / 127.0,
                                                     epsv[:], ALU.mult, ALU.add))
                w.run("V", lambda e: e.drain())
                w.run("V", lambda e, t2=ti: e.tensor_scalar(
                    shwd[:, t2:t2 + 1], shtmp[:], wscl[:, 2:3], None, ALU.mult),
                    inc="sV")
                w.wait("S", "sV", c["sV"])
                w.run("S", lambda e: e.activation(sc1[:], shtmp[:], AF.Ln))
                w.run("S", lambda e: e.drain())
                w.run("S", lambda e: e.activation(shinv[:], sc1[:], AF.Exp,
                                                  scale=-1.0))
                w.run("S", lambda e: e.drain())
                nrecip(shinv[:], shtmp[:], 1, inc="sS")
                shinv_done = c["sS"]
                for qt in range(4):
                    qsl = qt % 2
                    hbq = f32a[:, qsl * 2048:(qsl + 1) * 2048]
                    w.wait("Y", "sV", hqslot_free[qsl])
                    if tg == 0 and k == 0 and qt < 2:
                        w.wait("Y", "sT", last_p3b_mm)
                        w.wait("Y", "sV", last_p3b_v)
                        w.wait("Y", "sS", c["sS"])
                        w.wait("Y", "dH0", c["dH0"])
                        w.wait("Y", "dH1", c["dH1"])
                    hld = dma("Y", hbq,
                              h_d.ap()[:, ti * I + qt * 2048:ti * I + (qt + 1) * 2048],
                              "dH%d" % qsl)
                    w.wait("V", "dH%d" % qsl, hld)
                    w.wait("V", "sS", shinv_done)
                    if qt == 0:
                        w.wait("V", "sT", hq_freeT)
                    w.run("V", lambda e: e.drain())
                    w.run("V", lambda e, hb2=hbq: e.tensor_scalar(
                        rtmp, hb2, shinv[:], MAGIC, ALU.mult, ALU.add))
                    w.run("V", lambda e: e.drain())
                    w.run("V", lambda e, q2=qt: e.tensor_scalar(
                        hq[:, q2 * 2048:(q2 + 1) * 2048], rtmp, MAGIC, None,
                        ALU.subtract), inc="sV")
                    hqslot_free[qsl] = c["sV"]
                rounds_done = c["sV"]
                w.wait("T", "sV", rounds_done)
                if tg > 0:
                    w.wait("T", "sT", hqT_freeT)
                for g8 in range(8):
                    transpose_group(hq, g8 * 1024, 8, g8 % 2, hqT_ar[k], g8 * 1024)
                hq_freeT = c["sT"]
            hqT_readyV = c["sV"]
            w.wait("T", "sV", hqT_readyV)
            for og in range(4):
                for k in range(4):
                    ti = tg * 4 + k
                    w.wait("Y", "sV", y1l_free[k])
                    dma("Y", xbuf[:, k * 512:(k + 1) * 512],
                        y_d.ap()[ti * 128:(ti + 1) * 128, og * 512:(og + 1) * 512],
                        "dQ%d" % k)
                y1_all = [c["dQ%d" % k] for k in range(4)]

                def wd_issue(ic2, og2=og, tg2=tg):
                    ssl2 = ic2 % 4
                    eng2 = "Y" if ssl2 < 2 else "G"
                    w.wait(eng2, "sT", wdsl_freeT[ssl2])
                    if tg2 == 0 and og2 == 0 and ic2 < 4:
                        w.wait(eng2, "sV", last_quant_V)
                        w.wait(eng2, "sT", last_p3b_mm)
                    wd_lds[ssl2] = dma(
                        eng2, csb[:, ssl2 * 512:ssl2 * 512 + 512],
                        wdq_d.ap()[:, ic2 * 2048 + og2 * 512:
                                   ic2 * 2048 + (og2 + 1) * 512],
                        "dD%d" % ssl2)
                for ic2 in range(4):
                    wd_issue(ic2)
                for ic in range(ICN):
                    ssl = ic % 4
                    w.wait("T", "dD%d" % ssl, wd_lds[ssl])
                    if ic == 0:
                        for k in range(4):
                            w.wait("T", "sV", dpsum_freeV[k])
                    for k in range(4):
                        w.run("T", lambda e, k2=k, i2=ic, s2=ssl: e.matmul(
                            downbank[k2][:],
                            hqT_ar[k2][:, i2 * 128:(i2 + 1) * 128],
                            csb[:, s2 * 512:s2 * 512 + 512],
                            start=(i2 == 0), stop=(i2 == ICN - 1)),
                            inc="sT" if k == 3 else None)
                    wdsl_freeT[ssl] = c["sT"]
                    if ic + 4 < ICN:
                        wd_issue(ic + 4)
                mm_done = c["sT"]
                hqT_freeT = mm_done
                for k in range(4):
                    ti = tg * 4 + k
                    osl = k % 2
                    w.wait("V", "sT", mm_done)
                    w.wait("V", "dQ%d" % k, y1_all[k])
                    w.wait("V", "dO%d" % osl, out_dma[osl])
                    w.run("V", lambda e, k2=k, t2=ti, o3=osl:
                          e.scalar_tensor_tensor(
                              xbuf[:, 2048 + o3 * 512:2048 + (o3 + 1) * 512],
                              downbank[k2][:], shwd[:, t2:t2 + 1],
                              xbuf[:, k2 * 512:(k2 + 1) * 512],
                              ALU.mult, ALU.add), inc="sV")
                    dpsum_freeV[k] = c["sV"]
                    y1l_free[k] = c["sV"]
                    w.wait("Y", "sV", c["sV"])
                    out_dma[osl] = dma("Y", y_d.ap()[ti * 128:(ti + 1) * 128,
                                                     og * 512:(og + 1) * 512],
                                       xbuf[:, 2048 + osl * 512:
                                            2048 + (osl + 1) * 512],
                                       "dO%d" % osl)

        for sn in semnames:
            if sn.startswith("d"):
                w.wait("Y", sn, c[sn])

    with nc.Block() as block:
        @block.tensor
        def _(t):
            plan("T", t)

        @block.vector
        def _(v):
            plan("V", v)

        @block.scalar
        def _(s):
            plan("S", s)

        @block.sync
        def _(y):
            plan("Y", y)

        @block.gpsimd
        def _(g):
            plan("G", g)

    nc.finalize()
    return nc


def _host_constants():
    f = np.arange(32, dtype=np.float64)
    freq = 10000.0 ** (-f / 32.0)
    t = np.arange(T, dtype=np.float64)[:, None]
    phase = t * freq[None, :]
    cvals = np.cos(phase).astype(np.float32)
    svals = np.sin(phase).astype(np.float32)
    cs = np.empty((T, 2048), dtype=np.float32)
    cs[:, 0:1024] = np.tile(cvals, (1, 32))
    cs[:, 1024:2048] = np.tile(svals, (1, 32))
    r = np.arange(128)
    mask = np.where(r[None, :] <= r[:, None], 0.0, NEG).astype(np.float32)
    ident = np.eye(128, dtype=ml_dtypes.bfloat16)
    p64 = np.concatenate([np.arange(0, 64, 2), np.arange(1, 64, 2)])
    permq = np.concatenate([h * 64 + p64 for h in range(NH)])
    permk = np.concatenate([h * 64 + p64 for h in range(NKV)])
    return cs, mask, ident, permq, permk


def kernel(L0_attn_x, L0_attn_wq, L0_attn_wk, L0_attn_wv, L0_attn_wo,
           L0_ffn_w_gate, L0_ffn_w_up, L0_ffn_w_down):
    cs, mask, ident, permq, permk = _host_constants()
    x = np.asarray(L0_attn_x, dtype=np.float32)
    common = {
        "wq": np.ascontiguousarray(np.asarray(L0_attn_wq, np.float32)[:, permq]),
        "wk": np.ascontiguousarray(np.asarray(L0_attn_wk, np.float32)[:, permk]),
        "wv": np.ascontiguousarray(np.asarray(L0_attn_wv, np.float32)),
        "wo": np.ascontiguousarray(np.asarray(L0_attn_wo, np.float32)),
        "wgT": np.ascontiguousarray(np.asarray(L0_ffn_w_gate, np.float32).T),
        "wuT": np.ascontiguousarray(np.asarray(L0_ffn_w_up, np.float32).T),
        "wdT": np.ascontiguousarray(np.asarray(L0_ffn_w_down, np.float32).T),
        "cs": cs, "maskadd": mask, "ident": ident,
    }
    if "nc" not in _NC_CACHE:
        _NC_CACHE["nc"] = build_nc()
    nc = _NC_CACHE["nc"]
    in_maps = [dict(common, x=np.ascontiguousarray(x[c])) for c in range(8)]
    res = run_bass_kernel_spmd(nc, in_maps, core_ids=list(range(8)))
    return np.stack([np.asarray(r["y"], np.float32) for r in res.results], axis=0)


# version 10 (final)
# speedup vs baseline: 1.1357x; 1.0467x over previous
import os
import numpy as np
import ml_dtypes
import concourse.bass as bass
import concourse.mybir as mybir
from concourse.bass_utils import run_bass_kernel_spmd

F32 = mybir.dt.float32
BF16 = mybir.dt.bfloat16
I32 = mybir.dt.int32
AF = mybir.ActivationFunctionType
ALU = mybir.AluOpType
AXX = mybir.AxisListType.X

T = 2048
D = 2048
KV = 512
I = 8192
NH = 32
NKV = 8
NT = 16
DC = 16
ICN = 64
MAGIC = float(np.float32(1.5 * 2.0 ** 23))
NEG = -1.0e30
WN = float(2048 * 8192)
PHASE = int(os.environ.get("KPHASE", "5"))
KP2H = int(os.environ.get("KP2H", "32"))
KP2T = int(os.environ.get("KP2T", "16"))
KP2SERIAL = int(os.environ.get("KP2SERIAL", "0"))
KP2NOAV = int(os.environ.get("KP2NOAV", "0"))
KP2NOPT = int(os.environ.get("KP2NOPT", "0"))
KP2NOB = int(os.environ.get("KP2NOB", "0"))
KP2AMODE = int(os.environ.get("KP2AMODE", "0"))
KDUMP = os.environ.get("KDUMP", "")

_NC_CACHE = {}


class Walk:
    def __init__(self, nc, sems, active, engine):
        self.sems = sems
        self.active = active
        self.engine = engine
        self.c = {k: 0 for k in sems}

    def wait(self, eng, sname, val=None):
        v = self.c[sname] if val is None else val
        if eng == self.active and v > 0:
            self.engine.wait_ge(self.sems[sname], v)

    def run(self, eng, fn, inc=None, incn=1):
        if eng == self.active:
            ins = fn(self.engine)
            if inc is not None:
                ins.then_inc(self.sems[inc], incn)
        if inc is not None:
            self.c[inc] += incn


def build_nc():
    nc = bass.Bass("TRN2", target_bir_lowering=False)

    x_d = nc.dram_tensor("x", [T, D], F32, kind="ExternalInput")
    wq_d = nc.dram_tensor("wq", [D, D], F32, kind="ExternalInput")
    wk_d = nc.dram_tensor("wk", [D, KV], F32, kind="ExternalInput")
    wv_d = nc.dram_tensor("wv", [D, KV], F32, kind="ExternalInput")
    wo_d = nc.dram_tensor("wo", [D, D], F32, kind="ExternalInput")
    wg_d = nc.dram_tensor("wgT", [D, I], F32, kind="ExternalInput")
    wu_d = nc.dram_tensor("wuT", [D, I], F32, kind="ExternalInput")
    wd_d = nc.dram_tensor("wdT", [I, D], F32, kind="ExternalInput")
    cs_d = nc.dram_tensor("cs", [T, 2048], F32, kind="ExternalInput")
    mask_d = nc.dram_tensor("maskadd", [128, 128], F32, kind="ExternalInput")
    id_d = nc.dram_tensor("ident", [128, 128], BF16, kind="ExternalInput")
    y_d = nc.dram_tensor("y", [T, D], F32, kind="ExternalOutput")

    xnT_d = nc.dram_tensor("xnT_s", [128, NT * D], BF16, kind="Internal")
    xqT_d = nc.dram_tensor("xqT_s", [128, NT * D], BF16, kind="Internal")
    aT_d = nc.dram_tensor("aT_s", [128, NT * D], BF16, kind="Internal")
    h_d = nc.dram_tensor("h_s", [128, NT * I], F32, kind="Internal")
    wdq_d = nc.dram_tensor("wdq_s", [128, ICN * D], BF16, kind="Internal")
    parts_d = nc.dram_tensor("parts_s", [1, 384], F32, kind="Internal")

    A = nc.alloc_sbuf_tensor

    wbig = A("wbig", [128, 32 * 1024], BF16)
    ktdup = A("ktdup", [128, 16 * 1024], BF16)
    vt = A("vt", [128, 8 * 1024], BF16)
    pbuf = A("pbuf", [128, 8 * 1024], BF16)
    tb = A("tb", [128, 8704], BF16)
    xbuf = A("xbuf", [128, 2 * D], F32)
    f32a = A("f32a", [128, 2 * D], F32)
    csb = A("csb", [128, 2048], BF16)
    pjunk = A("pjunk", [128, 2048], BF16)
    wpre = A("wpre", [128, 2 * 1024], F32)
    atb = A("atb", [128, 2048], BF16)
    wdpc = A("wdpc", [128, 2048], BF16)

    wkv = wbig
    junk = f32a[:, 0:2048]
    xnf = f32a[:, 2048:4096]

    epsv = A("epsv", [128, 1], F32)
    onesr = A("onesr", [1, 128], F32)
    maskb = A("maskb", [128, 128], F32)
    identb = A("identb", [128, 128], BF16)
    msum = A("msum", [128, 1], F32)
    negm = A("negm", [128, 1], F32)
    ssq = A("ssq", [128, 1], F32)
    lt1 = A("lt1", [128, 1], F32)
    rstd = A("rstd", [128, 1], F32)
    negmr = A("negmr", [128, 1], F32)
    xmax = A("xmax", [128, 1], F32)
    sx_all = A("sx_all", [128, NT], F32)
    sxinv = A("sxinv", [128, 1], F32)
    mh = A("mh", [128, 1], F32)
    negb = A("negb", [128, 1], F32)
    lparts = A("lparts", [128, 4], F32)
    ljunk = A("ljunk", [128, 4], F32)
    lh = A("lh", [128, 1], F32)
    linv2 = A("linv2", [128, 2], F32)
    wsums = A("wsums", [128, 384], F32)
    wred = A("wred", [128, 3], F32)
    wrow = A("wrow", [1, 384], F32)
    wsum3 = A("wsum3", [128, 3], F32)
    wscl = A("wscl", [128, 3], F32)
    wsinv = A("wsinv", [128, 3], F32)
    sxwg = A("sxwg", [128, NT], F32)
    sxwu = A("sxwu", [128, NT], F32)
    hmaxp = A("hmaxp", [128, 256], F32)
    hmax = A("hmax", [128, 1], F32)
    shtmp = A("shtmp", [128, 1], F32)
    shinv = A("shinv", [128, 1], F32)
    shwd = A("shwd", [128, NT], F32)
    sc1 = A("sc1", [128, 1], F32)
    sc3 = A("sc3", [128, 3], F32)
    c2v = A("c2v", [128, 1], F32)
    c15v = A("c15v", [128, 1], F32)
    nt1 = A("nt1", [128, 3], F32)
    nt2 = A("nt2", [128, 3], F32)
    zerov = A("zerov", [128, 1], F32)
    onev = A("onev", [128, 1], F32)
    magicv = A("magicv", [128, 1], F32)
    negmagv = A("negmagv", [128, 1], F32)

    P = nc.alloc_psum_tensor
    pA = P("pA", [128, 512], F32)
    pB = P("pB", [128, 512], F32)
    pC = P("pC", [128, 512], F32)
    pD = P("pD", [128, 512], F32)
    pE = P("pE", [128, 1024], BF16)
    pF = P("pF", [128, 1024], BF16)
    pG = P("pG", [128, 512], F32)
    pH = P("pH", [128, 512], F32)

    semnames = ("sT", "sV", "sS", "dI", "dKV", "dCS", "dW", "dX0", "dX1",
                "dXN", "dXQ", "dP0", "dP1", "dR0", "dR1", "dAT", "dY10", "dY11",
                "dPA", "dD0", "dD1", "dD2", "dD3", "dH0", "dH1", "dQ0", "dQ1", "dQ2", "dQ3",
                "dO0", "dO1", "dXT")
    sems = {n: nc.alloc_semaphore(n) for n in semnames}

    def plan(active, engine):
        w = Walk(nc, sems, active, engine)
        c = w.c

        def dma(eng, out_ap, in_ap, sem):
            w.wait(eng, sem, c[sem])
            w.run(eng, lambda e: e.dma_start(out_ap, in_ap), inc=sem, incn=16)
            return c[sem]

        def nrecip(r_ap, s_ap, wdt, inc=None, nsteps=2):
            for it in range(nsteps):
                w.run("S", lambda e: e.mul(nt1[:, 0:wdt], r_ap, s_ap))
                w.run("S", lambda e: e.drain())
                w.run("S", lambda e: e.activation(nt2[:, 0:wdt], nt1[:, 0:wdt],
                                                  AF.Identity, bias=c2v[:],
                                                  scale=-1.0))
                w.run("S", lambda e: e.drain())
                last = it == nsteps - 1
                w.run("S", lambda e: e.mul(r_ap, r_ap, nt2[:, 0:wdt]),
                      inc=inc if last else None)
                if not last:
                    w.run("S", lambda e: e.drain())

        def nrsqrt(r_ap, v_ap, nsteps=2):
            for it in range(nsteps):
                w.run("S", lambda e: e.mul(nt1[:, 0:1], r_ap, r_ap))
                w.run("S", lambda e: e.drain())
                w.run("S", lambda e: e.mul(nt2[:, 0:1], nt1[:, 0:1], v_ap))
                w.run("S", lambda e: e.drain())
                w.run("S", lambda e: e.activation(nt1[:, 0:1], nt2[:, 0:1],
                                                  AF.Identity, bias=c15v[:],
                                                  scale=-0.5))
                w.run("S", lambda e: e.drain())
                w.run("S", lambda e: e.mul(r_ap, r_ap, nt1[:, 0:1]))
                w.run("S", lambda e: e.drain())

        w.run("V", lambda e: e.memset(epsv[:], 1e-5))
        w.run("V", lambda e: e.memset(onesr[:], 1.0))
        w.run("V", lambda e: e.memset(c2v[:], 2.0))
        w.run("V", lambda e: e.memset(c15v[:], 1.5))
        w.run("V", lambda e: e.memset(zerov[:], 0.0))
        w.run("V", lambda e: e.memset(onev[:], 1.0))
        w.run("V", lambda e: e.memset(magicv[:], MAGIC))
        w.run("V", lambda e: e.memset(negmagv[:], -MAGIC))
        w.run("V", lambda e: e.memset(atb[:], 0.0))
        w.run("V", lambda e: e.memset(hmaxp[:], 0.0), inc="sV")
        w.run("S", lambda e: e.add_instruction(mybir.InstLoadActFuncSet(
            name=e.bass.get_next_instruction_name(), act_func_set_id=6,
            ins=[], outs=[])))
        dma("Y", maskb[:], mask_d.ap(), "dI")
        ident_ld = dma("Y", identb[:], id_d.ap(), "dI")
        wkv3 = wkv.ap().rearrange("p (c o) -> p c o", o=512)
        dma("G", wkv3[:, 0:16, :], wk_d.ap().rearrange("(c p) o -> p c o", p=128), "dKV")
        wkv_ld = dma("G", wkv3[:, 16:32, :],
                     wv_d.ap().rearrange("(c p) o -> p c o", p=128), "dKV")

        pre_slices = []
        for si in range(128):
            pre_slices.append((wg_d, si // 8, (si % 8) * 1024, si))
        for si in range(128):
            pre_slices.append((wu_d, si // 8, (si % 8) * 1024, 128 + si))
        for si in range(128):
            pre_slices.append((wd_d, si // 2, (si % 2) * 1024, 256 + si))
        pre = {"iss": 0, "done": 0, "ld": [], "absS": [0, 0]}

        def prepass_step(n):
            for _ in range(n):
                while pre["iss"] < 384 and pre["iss"] - pre["done"] < 2:
                    k = pre["iss"]
                    wt, rc, co, col = pre_slices[k]
                    slot = wpre[:, (k % 2) * 1024:(k % 2) * 1024 + 1024]
                    src = wt.ap().rearrange("(c p) o -> p c o", p=128)[:, rc, co:co + 1024]
                    sn = "dP%d" % (k % 2)
                    w.wait("S", sn, c[sn])
                    w.wait("S", "sS", pre["absS"][k % 2])
                    w.run("S", lambda e, s=slot, r=src: e.dma_start(s, r),
                          inc=sn, incn=16)
                    pre["ld"].append((sn, c[sn]))
                    pre["iss"] += 1
                kd = pre["done"]
                if kd >= pre["iss"]:
                    return
                wt, rc, co, col = pre_slices[kd]
                slot = wpre[:, (kd % 2) * 1024:(kd % 2) * 1024 + 1024]
                w.wait("S", pre["ld"][kd][0], pre["ld"][kd][1])
                w.run("S", lambda e, s=slot, cc=col, kp=kd % 2: e.activation(
                    pjunk[:, kp * 1024:kp * 1024 + 1024], s, AF.Abs,
                    accum_out=wsums[:, cc:cc + 1]), inc="sS")
                pre["absS"][kd % 2] = c["sS"]
                pre["done"] += 1

        xload = [0, 0]
        xfree = [0, 0]
        cs_freeV = 0
        peF = {0: 0, 1: 0}
        xnT_dma = 0
        xqT_dma = 0
        pA_freeV = 0
        pB_freeV = 0
        junk_freeV = 0
        xnTt_freeT = 0
        kb_freeT = 0

        def transpose_group(src, col0, n, bankid, dst, dstcol):
            bank = pE if bankid == 0 else pF
            w.wait("T", "sV", peF[bankid])
            for k in range(n):
                w.run("T", lambda e, kk=k: e.transpose(
                    bank[:, kk * 128:(kk + 1) * 128],
                    src[:, col0 + kk * 128:col0 + (kk + 1) * 128],
                    identb[:]), inc="sT" if k == n - 1 else None)
            w.wait("V", "sT", c["sT"])
            w.run("V", lambda e: e.tensor_copy(dst[:, dstcol:dstcol + n * 128],
                                               bank[:, 0:n * 128]), inc="sV")
            peF[bankid] = c["sV"]

        xnb = tb[:, 0:2048]
        xqb = tb[:, 2048:4096]
        xnT_t = tb[:, 4096:6144]
        xqT_t = tb[:, 6144:8192]
        kb = tb[:, 8192:8704]

        def rope6(psum_bank, cs_off, nheads, dstv):
            w.run("V", lambda e: e.drain())
            pse = psum_bank.ap().rearrange("p (h q f) -> p h q f", h=nheads, q=2)
            cE = csb[:, cs_off:cs_off + nheads * 32].rearrange("p (h f) -> p h f", f=32)
            sEa = csb[:, 1024 + cs_off:1024 + cs_off + nheads * 32].rearrange(
                "p (h f) -> p h f", f=32)
            t1 = junk[:, 0:256].rearrange("p (h f) -> p h f", f=32)[:, 0:nheads, :]
            t2 = junk[:, 256:512].rearrange("p (h f) -> p h f", f=32)[:, 0:nheads, :]
            t3 = junk[:, 512:768].rearrange("p (h f) -> p h f", f=32)[:, 0:nheads, :]
            t4 = junk[:, 768:1024].rearrange("p (h f) -> p h f", f=32)[:, 0:nheads, :]
            w.run("V", lambda e: e.tensor_mul(t1, pse[:, :, 0, :], cE))
            w.run("V", lambda e: e.tensor_mul(t2, pse[:, :, 1, :], sEa))
            w.run("V", lambda e: e.tensor_mul(t3, pse[:, :, 0, :], sEa))
            w.run("V", lambda e: e.tensor_mul(t4, pse[:, :, 1, :], cE))
            w.run("V", lambda e: e.drain())
            w.run("V", lambda e: e.tensor_sub(dstv[:, :, 0, :], t1, t2))
            w.run("V", lambda e: e.tensor_add(dstv[:, :, 1, :], t3, t4), inc="sV")

        for i in range(NT):
            sl = i % 2
            w.wait("Y", "sS", xfree[sl])
            xload[sl] = dma("Y", xbuf[:, sl * D:(sl + 1) * D],
                            x_d.ap()[i * 128:(i + 1) * 128, :], "dX%d" % sl)
            w.wait("G", "sV", cs_freeV)
            csload = dma("G", csb[:], cs_d.ap()[i * 128:(i + 1) * 128, :], "dCS")

            xb = xbuf[:, sl * D:(sl + 1) * D]
            w.wait("V", "dX%d" % sl, xload[sl])
            w.run("V", lambda e, x2=xb: e.reduce_sum(msum[:], x2, axis=AXX))
            w.run("V", lambda e: e.drain())
            w.run("V", lambda e: e.tensor_scalar(negm[:], msum[:], -1.0 / D, None,
                                                 ALU.mult), inc="sV")
            w.wait("S", "sV", c["sV"])
            w.wait("S", "sV", junk_freeV)
            w.run("S", lambda e, x2=xb: e.activation(junk[:], x2, AF.Square,
                                                     bias=negm[:], accum_out=ssq[:]))
            w.run("S", lambda e: e.drain())
            w.run("S", lambda e: e.activation(lt1[:], ssq[:], AF.Ln, bias=epsv[:],
                                              scale=1.0 / D))
            w.run("S", lambda e: e.drain())
            w.run("S", lambda e: e.activation(rstd[:], lt1[:], AF.Exp, scale=-0.5))
            w.run("S", lambda e: e.drain())
            w.run("S", lambda e: e.activation(lt1[:], ssq[:], AF.Identity,
                                              bias=epsv[:], scale=1.0 / D))
            w.run("S", lambda e: e.drain())
            nrsqrt(rstd[:], lt1[:])
            w.run("S", lambda e: e.mul(negmr[:], negm[:], rstd[:]))
            w.run("S", lambda e: e.drain())
            w.run("S", lambda e, x2=xb: e.activation(xnf[:], x2, AF.Identity,
                                                     bias=negmr[:], scale=rstd[:]))
            w.wait("S", "sT", kb_freeT)
            w.run("S", lambda e, x2=xb: e.activation(xnb, x2, AF.Identity,
                                                     bias=negmr[:], scale=rstd[:]),
                  inc="sS")
            xn_done = c["sS"]
            xfree[sl] = xn_done
            w.wait("V", "sS", xn_done)
            w.run("V", lambda e: e.reduce_max(xmax[:], xnf[:], axis=AXX,
                                              apply_absolute_value=True))
            w.run("V", lambda e: e.drain())
            w.run("V", lambda e, ii=i: e.tensor_scalar(
                sx_all[:, ii:ii + 1], xmax[:], 1.0 / 127.0, epsv[:],
                ALU.mult, ALU.add), inc="sV")
            w.wait("S", "sV", c["sV"])
            w.run("S", lambda e, ii=i: e.activation(sc1[:], sx_all[:, ii:ii + 1], AF.Ln))
            w.run("S", lambda e: e.drain())
            w.run("S", lambda e: e.activation(sxinv[:], sc1[:], AF.Exp, scale=-1.0))
            w.run("S", lambda e: e.drain())
            nrecip(sxinv[:], sx_all[:, i:i + 1], 1, inc="sS")
            w.wait("V", "sS", c["sS"])
            w.run("V", lambda e: e.tensor_scalar(junk[:], xnf[:], sxinv[:], MAGIC,
                                                 ALU.mult, ALU.add))
            w.run("V", lambda e: e.drain())
            w.run("V", lambda e: e.tensor_scalar(xqb, junk[:], MAGIC, None,
                                                 ALU.subtract), inc="sV")
            xq_done = c["sV"]

            w.wait("T", "dI", ident_ld)
            w.wait("V", "dXN", xnT_dma)
            w.wait("V", "dXQ", xqT_dma)
            w.wait("T", "sT", xnTt_freeT)
            w.wait("T", "sS", xn_done)
            transpose_group(xnb, 0, 8, 0, xnT_t, 0)
            transpose_group(xnb, 1024, 8, 1, xnT_t, 1024)
            w.wait("T", "sV", xq_done)
            transpose_group(xqb, 0, 8, 0, xqT_t, 0)
            transpose_group(xqb, 1024, 8, 1, xqT_t, 1024)
            w.wait("Y", "sV", c["sV"])
            xnT_dma = dma("Y", xnT_d.ap()[:, i * D:(i + 1) * D], xnT_t, "dXN")
            xqT_dma = dma("Y", xqT_d.ap()[:, i * D:(i + 1) * D], xqT_t, "dXQ")

            w.wait("T", "dKV", wkv_ld)
            w.wait("T", "sV", pA_freeV)
            for cc in range(DC):
                w.run("T", lambda e, k=cc: e.matmul(
                    pA[:], xnT_t[:, k * 128:(k + 1) * 128],
                    wkv[:, k * 512:(k + 1) * 512],
                    start=(k == 0), stop=(k == DC - 1)),
                    inc="sT" if cc == DC - 1 else None)
            w.wait("V", "sT", c["sT"])
            w.wait("V", "dCS", csload)
            kbv = kb.rearrange("p (h q f) -> p h q f", h=8, q=2)
            rope6(pA, 0, 8, kbv)
            pA_freeV = c["sV"]
            cs_freeV = c["sV"]
            junk_freeV = c["sV"]
            w.wait("T", "sV", c["sV"])
            w.wait("T", "sV", peF[0])
            for g in range(8):
                w.run("T", lambda e, gg=g: e.transpose(
                    pE[0:64, gg * 128:gg * 128 + 128], kb[:, gg * 64:(gg + 1) * 64],
                    identb[:]))
                w.run("T", lambda e, gg=g: e.transpose(
                    pE[64:128, gg * 128:gg * 128 + 128], kb[:, gg * 64:(gg + 1) * 64],
                    identb[:], tile_position=(0, 64)),
                    inc="sT" if g == 7 else None)
            kb_freeT = c["sT"]
            w.wait("V", "sT", c["sT"])
            ktv = ktdup.ap().rearrange("p (g t) -> p g t", t=2048)
            w.run("V", lambda e, ii=i: e.tensor_copy(
                ktv[:, :, ii * 128:(ii + 1) * 128],
                pE.ap().rearrange("p (g t) -> p g t", t=128)), inc="sV")
            peF[0] = c["sV"]

            w.wait("T", "sV", pB_freeV)
            for cc in range(DC):
                w.run("T", lambda e, k=cc: e.matmul(
                    pB[:], xnT_t[:, k * 128:(k + 1) * 128],
                    wkv[:, 8192 + k * 512:8192 + (k + 1) * 512],
                    start=(k == 0), stop=(k == DC - 1)),
                    inc="sT" if cc == DC - 1 else None)
            xnTt_freeT = c["sT"]
            w.wait("V", "sT", c["sT"])
            w.run("V", lambda e, ii=i: e.tensor_copy(vt[:, ii * 512:(ii + 1) * 512],
                                                     pB[:]), inc="sV")
            pB_freeV = c["sV"]
            prepass_step(8)

        if PHASE < 2:
            if KDUMP == "aT":
                for sn in semnames:
                    if sn.startswith("d") and sn != "dW":
                        w.wait("G", sn, c[sn])
                dma("G", y_d.ap().rearrange("(c p) o -> p c o", p=128),
                    aT_d.ap().rearrange("p (t o) -> p t o", o=2048), "dW")
            elif KDUMP == "xnT" and "2" == "2":
                for sn in semnames:
                    if sn.startswith("d") and sn != "dW":
                        w.wait("G", sn, c[sn])
                dma("G", y_d.ap().rearrange("(c p) o -> p c o", p=128),
                    xnT_d.ap().rearrange("p (t o) -> p t o", o=2048), "dW")
            for sn in semnames:
                if sn.startswith("d"):
                    w.wait("Y", sn, c[sn])
            return
        w.wait("G", "sT", c["sT"])
        wq_ld = dma("G", wbig.ap().rearrange("p (c o) -> p c o", o=2048),
                    wq_d.ap().rearrange("(c p) o -> p c o", p=128), "dW")

        xnr_ld = [0, 0]
        xnr_freeT = [c["sT"], c["sT"]]
        praw_free = [0, 0]
        pf_freeT = 0
        pt_freeT = 0
        av_freeV = [0, 0]
        mask_readV = 0
        mh_freeS = 0
        atb_dma = 0
        qbank_free = [("sV", c["sV"]), ("sV", c["sV"])]
        qb_freeT = c["sT"]
        rope_og = {0: 0, 1: 0}
        wq_lastT = 0

        qb = tb[:, 4096:6144]
        qT_t = tb[:, 6144:8192]
        praw01 = (pbuf[:, 0:2048], pbuf[:, 2048:4096])
        pf = pbuf[:, 4096:6144]
        pt = pbuf[:, 6144:8192]

        for i in range(min(NT, KP2T)):
            sl = i % 2
            xnr = tb[:, sl * 2048:(sl + 1) * 2048]
            w.wait("Y", "sT", xnr_freeT[sl])
            xnr_ld[sl] = dma("Y", xnr, xnT_d.ap()[:, i * D:(i + 1) * D],
                             "dR%d" % sl)
            w.wait("G", "sV", cs_freeV)
            csload = dma("G", csb[:], cs_d.ap()[i * 128:(i + 1) * 128, :], "dCS")

            for og in range(4):
                bank = pA if og % 2 == 0 else pB
                w.wait("T", "dW", wq_ld)
                w.wait("T", "dR%d" % sl, xnr_ld[sl])
                kind, val = qbank_free[og % 2]
                w.wait("T", kind, val)
                w.wait("T", "sV", rope_og[og % 2])
                for cc in range(DC):
                    w.run("T", lambda e, k=cc, b2=bank, o=og, xn2=xnr: e.matmul(
                        b2[:], xn2[:, k * 128:(k + 1) * 128],
                        wbig[:, k * 2048 + o * 512:k * 2048 + (o + 1) * 512],
                        start=(k == 0), stop=(k == DC - 1)),
                        inc="sT" if cc == DC - 1 else None)
                w.wait("V", "sT", c["sT"])
                w.wait("V", "dCS", csload)
                if og == 0:
                    w.wait("V", "sT", qb_freeT)
                    w.wait("V", "sV", junk_freeV)
                    w.wait("V", "dXN", xnT_dma)
                    w.wait("V", "dXQ", xqT_dma)
                qbv = qb[:, og * 512:(og + 1) * 512].rearrange(
                    "p (h q f) -> p h q f", h=8, q=2)
                rope6(bank, og * 256, 8, qbv)
                rope_og[og % 2] = c["sV"]
            if i == NT - 1:
                wq_lastT = c["sT"]
            rope_done = c["sV"]
            cs_freeV = rope_done
            junk_freeV = rope_done
            qbank_free = [("sV", rope_done), ("sV", rope_done)]
            w.wait("T", "sV", rope_done)
            transpose_group(qb, 0, 8, 0, qT_t, 0)
            transpose_group(qb, 1024, 8, 1, qT_t, 1024)
            qT_ready = c["sV"]

            nj = i + 1
            chunk_bases = []
            b = nj
            while b > 0:
                lo = max(0, b - 4)
                chunk_bases.append((lo, b))
                b = lo
            slot_free = {0: ("sV", qT_ready), 1: ("sV", qT_ready)}

            def stage_a(h):
                nonlocal mask_readV, mh_freeS
                par = h % 2
                praw = praw01[par]
                w.wait("S", "sV", praw_free[par])
                for ci, (lo, hi) in enumerate(chunk_bases):
                    bank = pA if ci % 2 == 0 else pB
                    kind, val = slot_free[ci % 2]
                    w.wait("T", kind, val)
                    if ci == 0:
                        w.wait("T", "sV", mask_readV)
                    lq = qT_t[par * 64:par * 64 + 64,
                              (h // 2) * 128:(h // 2) * 128 + 128]
                    for j in range(lo, hi):
                        dest = (pG[:, 0:128] if j == i
                                else bank[:, (j - lo) * 128:(j - lo + 1) * 128])
                        w.run("T", lambda e, jj=j, d2=dest, q2=lq, p2=par, h2=h:
                              e.matmul(d2, q2,
                                       ktdup[p2 * 64:p2 * 64 + 64,
                                             (h2 // 4) * 2048 + jj * 128:
                                             (h2 // 4) * 2048 + (jj + 1) * 128],
                                       start=True, stop=True),
                              inc="sT" if j == hi - 1 else None)
                    qkc = c["sT"]
                    if KP2AMODE == 1:
                        slot_free[ci % 2] = ("sT", qkc)
                        continue
                    if ci == 0:
                        sdiag = i - lo
                        w.wait("V", "sT", qkc)
                        w.wait("V", "sS", mh_freeS)
                        w.run("V", lambda e, b2=bank, sd=sdiag: e.tensor_add(
                            b2[:, sd * 128:(sd + 1) * 128], pG[:, 0:128], maskb[:]))
                        w.run("V", lambda e: e.drain())
                        w.run("V", lambda e, b2=bank, sd=sdiag: e.reduce_max(
                            mh[:], b2[:, sd * 128:(sd + 1) * 128], axis=AXX))
                        w.run("V", lambda e: e.drain(), inc="sV")
                        mask_readV = c["sV"]
                        w.wait("S", "sV", mask_readV)
                        w.run("S", lambda e: e.mul(negb[:], mh[:], -0.125))
                        w.run("S", lambda e: e.drain())
                    else:
                        w.wait("S", "sT", qkc)
                    if KP2AMODE == 2:
                        slot_free[ci % 2] = ("sV", mask_readV)
                        continue
                    nsl = hi - lo
                    if KP2AMODE == 3:
                        w.run("S", lambda e, b2=bank, l2=lo, n2=nsl, pr=praw:
                              e.activation(pr[:, l2 * 128:(l2 + n2) * 128],
                                           b2[:, 0:n2 * 128], AF.Exp,
                                           bias=negb[:], scale=0.125), inc="sS")
                    else:
                        w.run("S", lambda e, b2=bank, l2=lo, n2=nsl, cc2=ci, pr=praw:
                              e.activation(pr[:, l2 * 128:(l2 + n2) * 128],
                                           b2[:, 0:n2 * 128], AF.Exp,
                                           bias=negb[:], scale=0.125,
                                           accum_out=lparts[:, cc2:cc2 + 1]),
                              inc="sS")
                    slot_free[ci % 2] = ("sS", c["sS"])
                    if ci == 0:
                        mh_freeS = c["sS"]
                nch = len(chunk_bases)
                if KP2AMODE == 5:
                    w.run("S", lambda e: e.drain())
                    w.run("S", lambda e, n2=nch: e.activation(
                        ljunk[:, 0:n2], lparts[:, 0:n2], AF.Copy, accum_out=lh[:]))
                    w.run("S", lambda e: e.memzero(sc1[:]), inc="sS")
                    return c["sS"]
                if KP2AMODE == 6:
                    w.run("S", lambda e: e.activation(sc1[:], epsv[:], AF.Ln))
                    w.run("S", lambda e: e.drain())
                    w.run("S", lambda e, p2=par: e.activation(
                        linv2[:, p2:p2 + 1], sc1[:], AF.Exp, scale=-1.0), inc="sS")
                    return c["sS"]
                if KP2AMODE == 4:
                    w.run("S", lambda e: e.memzero(sc1[:]), inc="sS")
                    return c["sS"]
                if KP2AMODE in (1, 2):
                    w.run("S", lambda e: e.memzero(sc1[:]), inc="sS")
                    return c["sS"]
                w.run("S", lambda e: e.drain())
                w.run("S", lambda e, n2=nch: e.activation(
                    ljunk[:, 0:n2], lparts[:, 0:n2], AF.Copy, accum_out=lh[:]))
                w.run("S", lambda e: e.drain())
                w.run("S", lambda e: e.activation(sc1[:], lh[:], AF.Ln))
                w.run("S", lambda e: e.drain())
                w.run("S", lambda e, p2=par: e.activation(
                    linv2[:, p2:p2 + 1], sc1[:], AF.Exp, scale=-1.0), inc="sS")
                return c["sS"]

            def stage_b(h, linv_done):
                nonlocal pf_freeT, pt_freeT, atb_dma
                if KP2NOB:
                    return
                par = h % 2
                praw = praw01[par]
                w.wait("V", "sS", linv_done)
                w.wait("V", "sT", pf_freeT)
                w.run("V", lambda e, p2=par, pr=praw: e.tensor_scalar(
                    pf[:, 0:nj * 128], pr[:, 0:nj * 128], linv2[:, p2:p2 + 1],
                    None, ALU.mult), inc="sV")
                scale_done = c["sV"]
                praw_free[par] = scale_done
                if KP2NOPT:
                    return
                g0 = 0
                bankid = 0
                last_ptev = 0
                while g0 < nj:
                    n = min(8, nj - g0)
                    bank = pE if bankid == 0 else pF
                    w.wait("T", "sV", scale_done)
                    w.wait("T", "sV", peF[bankid])
                    w.wait("T", "sT", pt_freeT)
                    for k in range(n):
                        w.run("T", lambda e, kk=k, g=g0, b2=bank: e.transpose(
                            b2[:, kk * 128:(kk + 1) * 128],
                            pf[:, (g + kk) * 128:(g + kk + 1) * 128], identb[:]),
                            inc="sT" if k == n - 1 else None)
                    w.wait("V", "sT", c["sT"])
                    w.run("V", lambda e, g=g0, n2=n, b2=bank: e.tensor_copy(
                        pt[:, g * 128:(g + n2) * 128], b2[:, 0:n2 * 128]), inc="sV")
                    peF[bankid] = c["sV"]
                    last_ptev = c["sV"]
                    bankid ^= 1
                    g0 += n
                pf_freeT = c["sT"]
                if KP2NOAV:
                    return
                avbank = pC if par == 0 else pD
                w.wait("T", "sV", last_ptev)
                w.wait("T", "sV", av_freeV[par])
                for j in range(nj):
                    w.run("T", lambda e, jj=j, b2=avbank, p2=par, h2=h: e.matmul(
                        b2[p2 * 64:p2 * 64 + 64, 0:128],
                        vt[:, jj * 512 + (h2 // 4) * 64:jj * 512 + (h2 // 4) * 64 + 64],
                        pt[:, jj * 128:(jj + 1) * 128],
                        start=(jj == 0), stop=(jj == nj - 1),
                        tile_position=(0, 64 * p2)),
                        inc="sT" if j == nj - 1 else None)
                pt_freeT = c["sT"]
                w.wait("V", "sT", c["sT"])
                w.wait("V", "dAT", atb_dma)
                w.run("V", lambda e, b2=avbank, p2=par, h2=h: e.tensor_copy(
                    atb[p2 * 64:p2 * 64 + 64, (h2 // 2) * 128:(h2 // 2) * 128 + 128],
                    b2[p2 * 64:p2 * 64 + 64, 0:128]), inc="sV")
                av_freeV[par] = c["sV"]

            linv_at = {}
            nheads = KP2H
            if KP2SERIAL:
                for h in range(nheads):
                    linv_at[h] = stage_a(h)
                    stage_b(h, linv_at[h])
                    if h % 8 == 0:
                        prepass_step(2)
            else:
                for h in range(nheads + 1):
                    if h < nheads:
                        linv_at[h] = stage_a(h)
                    if h >= 1:
                        stage_b(h - 1, linv_at[h - 1])
                    if h % 8 == 0:
                        prepass_step(2)
            qbank_free = [slot_free[0], slot_free[1]]
            xnr_freeT[sl] = c["sT"]
            qb_freeT = c["sT"]
            w.wait("Y", "sV", c["sV"])
            atb_dma = dma("Y", aT_d.ap()[:, i * D:(i + 1) * D], atb[:], "dAT")

        while pre["done"] < 384:
            prepass_step(4)

        if PHASE < 3:
            if KDUMP == "aT":
                for sn in semnames:
                    if sn.startswith("d") and sn != "dW":
                        w.wait("G", sn, c[sn])
                dma("G", y_d.ap().rearrange("(c p) o -> p c o", p=128),
                    aT_d.ap().rearrange("p (t o) -> p t o", o=2048), "dW")
            elif KDUMP == "xnT" and "3" == "2":
                for sn in semnames:
                    if sn.startswith("d") and sn != "dW":
                        w.wait("G", sn, c[sn])
                dma("G", y_d.ap().rearrange("(c p) o -> p c o", p=128),
                    xnT_d.ap().rearrange("p (t o) -> p t o", o=2048), "dW")
            for sn in semnames:
                if sn.startswith("d"):
                    w.wait("Y", sn, c[sn])
            return
        w.wait("G", "sT", wq_lastT)
        wo_ld_og = []
        for og in range(4):
            wo_ld_og.append(dma(
                "G",
                wbig.ap().rearrange("p (c o) -> p c o", o=2048)[
                    :, :, og * 512:(og + 1) * 512],
                wo_d.ap().rearrange("(c p) o -> p c o", p=128)[
                    :, :, og * 512:(og + 1) * 512],
                "dW"))

        w.wait("V", "sS", c["sS"])
        for t in range(3):
            w.run("V", lambda e, tt=t: e.reduce_sum(
                wred[:, tt:tt + 1], wsums[:, tt * 128:(tt + 1) * 128], axis=AXX))
        w.run("V", lambda e: e.drain(), inc="sV")
        w.wait("Y", "sV", c["sV"])
        pd128 = parts_d.ap().rearrange("o (p t) -> (o p) t", t=3)
        dma("Y", pd128, wred[:], "dPA")
        w.wait("Y", "dPA", c["dPA"])
        dma("Y", wrow[:], parts_d.ap(), "dPA")
        w.wait("T", "dPA", c["dPA"])
        w.wait("T", "sV", 1)
        w.run("T", lambda e: e.matmul(pH[:, 0:384], onesr[:], wrow[:],
                                      start=True, stop=True), inc="sT")
        w.wait("V", "sT", c["sT"])
        pHv = pH[:, 0:384].rearrange("p (k t) -> p k t", t=3)
        for t in range(3):
            w.run("V", lambda e, tt=t: e.reduce_sum(
                wsum3[:, tt:tt + 1], pHv[:, :, tt], axis=AXX))
        w.run("V", lambda e: e.drain())
        w.run("V", lambda e: e.tensor_scalar(wscl[:], wsum3[:], 1.0 / WN, epsv[:],
                                             ALU.mult, ALU.add), inc="sV")
        w.wait("S", "sV", c["sV"])
        w.run("S", lambda e: e.activation(sc3[:], wscl[:], AF.Ln))
        w.run("S", lambda e: e.drain())
        w.run("S", lambda e: e.activation(wsinv[:], sc3[:], AF.Exp, scale=-1.0))
        w.run("S", lambda e: e.drain())
        for t in range(3):
            nrecip(wsinv[:, t:t + 1], wscl[:, t:t + 1], 1,
                   inc="sS" if t == 2 else None)
            if t < 2:
                w.run("S", lambda e: e.drain())
        wscales_done = c["sS"]

        aT_ld = [0, 0]
        aT_freeT = [0, 0]
        y1_dma = [0, 0]
        x2load = [0, 0]
        x2_freeV = [0, 0]
        evacV = {0: 0, 1: 0}
        w.wait("Y", "sS", c["sS"])
        w.wait("Y", "sV", c["sV"])
        w.wait("Y", "dAT", c["dAT"])
        for i in range(NT):
            sl = i % 2
            aT = csb[:] if sl == 0 else atb[:]
            w.wait("Y", "sT", aT_freeT[sl])
            aT_ld[sl] = dma("Y", aT, aT_d.ap()[:, i * D:(i + 1) * D], "dR%d" % sl)
            w.wait("Y", "sV", x2_freeV[sl])
            x2load[sl] = dma("Y", xbuf[:, sl * D:(sl + 1) * D],
                             x_d.ap()[i * 128:(i + 1) * 128, :], "dX%d" % sl)
            y1 = f32a[:, sl * 2048:(sl + 1) * 2048]
            for og in range(4):
                bank = pA if og % 2 == 0 else pB
                w.wait("T", "dW", wo_ld_og[og])
                w.wait("T", "dR%d" % sl, aT_ld[sl])
                w.wait("T", "sV", evacV[og % 2])
                for cc in range(DC):
                    w.run("T", lambda e, k=cc, b2=bank, o=og, a2=aT: e.matmul(
                        b2[:], a2[:, k * 128:(k + 1) * 128],
                        wbig[:, k * 2048 + o * 512:k * 2048 + (o + 1) * 512],
                        start=(k == 0), stop=(k == DC - 1)),
                        inc="sT" if cc == DC - 1 else None)
                w.wait("V", "sT", c["sT"])
                w.wait("V", "dX%d" % sl, x2load[sl])
                w.wait("V", "dY1%d" % sl, y1_dma[sl])
                w.run("V", lambda e, b2=bank, o=og, s2=sl, yy=y1:
                      e.scalar_tensor_tensor(
                          yy[:, o * 512:(o + 1) * 512], b2[:], 1.0,
                          xbuf[:, s2 * D + o * 512:s2 * D + (o + 1) * 512],
                          ALU.mult, ALU.add), inc="sV")
                evacV[og % 2] = c["sV"]
            aT_freeT[sl] = c["sT"]
            x2_freeV[sl] = c["sV"]
            w.wait("Y", "sV", c["sV"])
            y1_dma[sl] = dma("Y", y_d.ap()[i * 128:(i + 1) * 128, :], y1,
                             "dY1%d" % sl)

        if PHASE < 4:
            if KDUMP == "aT":
                for sn in semnames:
                    if sn.startswith("d") and sn != "dW":
                        w.wait("G", sn, c[sn])
                dma("G", y_d.ap().rearrange("(c p) o -> p c o", p=128),
                    aT_d.ap().rearrange("p (t o) -> p t o", o=2048), "dW")
            elif KDUMP == "xnT" and "4" == "2":
                for sn in semnames:
                    if sn.startswith("d") and sn != "dW":
                        w.wait("G", sn, c[sn])
                dma("G", y_d.ap().rearrange("(c p) o -> p c o", p=128),
                    xnT_d.ap().rearrange("p (t o) -> p t o", o=2048), "dW")
            for sn in semnames:
                if sn.startswith("d"):
                    w.wait("Y", sn, c[sn])
            return
        w.wait("Y", "sT", c["sT"])
        xqT_ld = dma("Y", wbig[:], xqT_d.ap(), "dXT")
        w.wait("V", "sS", wscales_done)
        w.wait("V", "sV", c["sV"])
        w.wait("V", "dAT", atb_dma)
        w.wait("V", "dY10", c["dY10"])
        w.wait("V", "dY11", c["dY11"])
        for t in range(NT):
            w.run("V", lambda e, tt=t: e.tensor_scalar(
                sxwg[:, tt:tt + 1], sx_all[:, tt:tt + 1], wscl[:, 0:1], None, ALU.mult))
            w.run("V", lambda e, tt=t: e.tensor_scalar(
                sxwu[:, tt:tt + 1], sx_all[:, tt:tt + 1], wscl[:, 1:2], None, ALU.mult),
                inc="sV" if t == NT - 1 else None)
        w.run("S", lambda e: e.preload_act_table(AF.Silu))

        qslot_free = [("sV", 0), ("sV", 0)]
        guq_lastT = [0, 0]
        wdpc_wr = [0, 0]

        def wguq_bufs(oc):
            if oc % 2 == 0:
                return ktdup[:, 0:8192], ktdup[:, 8192:16384]
            return vt[:, 0:8192], pbuf[:, 0:8192]

        qjobs = []
        for oc in range(16):
            for rc in range(DC):
                qjobs.append(("g", oc, rc))
            for rc in range(DC):
                qjobs.append(("u", oc, rc))
            for kk in range(8):
                qjobs.append(("d", oc * 8 + kk, 0))
        qs = {"k": 0}
        wg_ready = [0] * 16
        wu_ready = [0] * 16

        def quant_pump(n, cur_oc=99):
            for _ in range(n):
                if qs["k"] >= len(qjobs):
                    return
                if qs["k"] // 40 > cur_oc + 1:
                    return
                kind, a1, a2 = qjobs[qs["k"]]
                kq = qs["k"]
                qs["k"] += 1
                qsl = kq % 2
                slot = wpre[:, qsl * 1024:qsl * 1024 + 1024]
                if kind in ("g", "u"):
                    n2 = 512
                    wsel = 0 if kind == "g" else 1
                    oc2, rc = a1, a2
                    wt = wg_d if kind == "g" else wu_d
                    srcp = wt.ap().rearrange("(c p) o -> p c o", p=128)[
                        :, rc, oc2 * 512:oc2 * 512 + 512]
                    extra_waitT = guq_lastT[oc2 % 2]
                    dstg, dstu = wguq_bufs(oc2)
                    dst = dstg if kind == "g" else dstu
                    dstcol = rc * 512
                    wd_slot = None
                else:
                    idx = a1
                    ic, half = idx // 2, idx % 2
                    n2 = 1024
                    wsel = 2
                    srcp = wd_d.ap().rearrange("(c p) o -> p c o", p=128)[
                        :, ic, half * 1024:half * 1024 + 1024]
                    extra_waitT = 0
                    wd_slot = idx % 2
                    dst = wdpc
                    dstcol = wd_slot * 1024
                qa = xbuf[:, 1536:1536 + n2]
                qb2 = atb[:, 0:n2]
                qk, qv = qslot_free[qsl]
                w.wait("Y", qk, qv)
                qsn = "dP%d" % qsl
                ld = dma("Y", slot[:, 0:n2], srcp, qsn)
                w.wait("V", qsn, ld)
                if extra_waitT:
                    w.wait("V", "sT", extra_waitT)
                if wd_slot is not None:
                    w.wait("V", "dD%d" % wd_slot, wdpc_wr[wd_slot])
                w.run("V", lambda e, s2=slot, q2=qa, ws=wsel, n3=n2: e.tensor_scalar(
                    q2, s2[:, 0:n3], wsinv[:, ws:ws + 1], MAGIC, ALU.mult, ALU.add))
                w.run("V", lambda e: e.drain())
                w.run("V", lambda e, q2=qa, b2=qb2: e.tensor_scalar(
                    b2, q2, MAGIC, 1.0, ALU.subtract, ALU.min))
                w.run("V", lambda e: e.drain())
                w.run("V", lambda e, b2=qb2, d2=dst, dc2=dstcol, n3=n2:
                      e.tensor_scalar(d2[:, dc2:dc2 + n3], b2, -1.0, None, ALU.max),
                      inc="sV")
                qslot_free[qsl] = ("sV", c["sV"])
                if kind == "g" and a2 == DC - 1:
                    wg_ready[a1] = c["sV"]
                elif kind == "u" and a2 == DC - 1:
                    wu_ready[a1] = c["sV"]
                elif kind == "d":
                    idx = a1
                    ic, half = idx // 2, idx % 2
                    w.wait("Y", "sV", c["sV"])
                    wdpc_wr[idx % 2] = dma(
                        "Y", wdq_d.ap()[:, ic * 2048 + half * 1024:
                                        ic * 2048 + half * 1024 + 1024],
                        wdpc[:, (idx % 2) * 1024:(idx % 2) * 1024 + 1024],
                        "dD%d" % (idx % 2))

        w.wait("S", "sV", c["sV"])
        w.wait("S", "dY10", c["dY10"])
        w.wait("S", "dY11", c["dY11"])
        quant_pump(40, -1)

        gate_bank = [pA, pC]
        up_bank = [pB, pD]
        bfree = {"pA": ("sV", 0), "pB": ("sV", 0), "pC": ("sV", 0),
                 "pD": ("sV", 0)}
        bname = {id(pA): "pA", id(pB): "pB", id(pC): "pC", id(pD): "pD"}
        hdma = [0, 0]
        hmax_prev = [0, 0]

        for oc in range(16):
            wgq_t, wuq_t = wguq_bufs(oc)
            for ti in range(NT):
                gb = gate_bank[ti % 2]
                ub = up_bank[ti % 2]
                w.wait("T", "dXT", xqT_ld)
                w.wait("T", "sV", wg_ready[oc])
                bk, bv = bfree[bname[id(gb)]]
                w.wait("T", bk, bv)
                for cc in range(DC):
                    w.run("T", lambda e, k=cc, b2=gb, t2=ti, wg2=wgq_t: e.matmul(
                        b2[:], wbig[:, t2 * 2048 + k * 128:t2 * 2048 + (k + 1) * 128],
                        wg2[:, k * 512:(k + 1) * 512],
                        start=(k == 0), stop=(k == DC - 1)),
                        inc="sT" if cc == DC - 1 else None)
                gmm = c["sT"]
                w.wait("T", "sV", wu_ready[oc])
                bk, bv = bfree[bname[id(ub)]]
                w.wait("T", bk, bv)
                for cc in range(DC):
                    w.run("T", lambda e, k=cc, b2=ub, t2=ti, wu2=wuq_t: e.matmul(
                        b2[:], wbig[:, t2 * 2048 + k * 128:t2 * 2048 + (k + 1) * 128],
                        wu2[:, k * 512:(k + 1) * 512],
                        start=(k == 0), stop=(k == DC - 1)),
                        inc="sT" if cc == DC - 1 else None)
                umm = c["sT"]
                guq_lastT[oc % 2] = umm
                quant_pump(3, oc)
                gf = xbuf[:, 0:512]
                sgf = xbuf[:, 512:1024]
                uf = xbuf[:, 1024:1536]
                hsl = ti % 2
                hout = f32a[:, hsl * 512:hsl * 512 + 512]
                w.wait("S", "sT", gmm)
                w.run("S", lambda e, b2=gb, t2=ti, g2=gf: e.activation(
                    g2, b2[:], AF.Identity, bias=zerov[:],
                    scale=sxwg[:, t2:t2 + 1]), inc="sS")
                bfree[bname[id(gb)]] = ("sS", c["sS"])
                w.run("S", lambda e: e.drain())
                w.wait("S", "sV", hmax_prev[1 - hsl])
                w.run("S", lambda e, g2=gf, s2=sgf: e.activation(s2, g2, AF.Silu))
                w.run("S", lambda e: e.drain())
                w.wait("S", "sT", umm)
                w.run("S", lambda e, b2=ub, t2=ti, u2=uf: e.activation(
                    u2, b2[:], AF.Identity, bias=zerov[:],
                    scale=sxwu[:, t2:t2 + 1]), inc="sS")
                bfree[bname[id(ub)]] = ("sS", c["sS"])
                su_done = c["sS"]
                w.wait("V", "sS", su_done)
                w.wait("V", "dH%d" % hsl, hdma[hsl])
                w.run("V", lambda e, ho=hout, s2=sgf, u2=uf: e.tensor_mul(ho, s2, u2))
                w.run("V", lambda e: e.drain())
                w.run("V", lambda e, o2=oc, t2=ti, ho=hout: e.reduce_max(
                    hmaxp[:, o2 * 16 + t2:o2 * 16 + t2 + 1], ho, axis=AXX,
                    apply_absolute_value=True), inc="sV")
                hmax_prev[hsl] = c["sV"]
                w.wait("Y", "sV", c["sV"])
                hdma[hsl] = dma("Y", h_d.ap()[:, ti * I + oc * 512:
                                              ti * I + (oc + 1) * 512], hout,
                                "dH%d" % hsl)
        quant_pump(len(qjobs))

        if PHASE < 5:
            for sn in semnames:
                if sn.startswith("d"):
                    w.wait("Y", sn, c[sn])
            return
        last_quant_V = c["sV"]
        last_quant_S = c["sS"]
        w.run("S", lambda e: e.add_instruction(mybir.InstLoadActFuncSet(
            name=e.bass.get_next_instruction_name(), act_func_set_id=6,
            ins=[], outs=[])))
        downbank = [pA, pB, pC, pD]
        hqT_ar = [ktdup[:, 0:8192], ktdup[:, 8192:16384], vt[:, 0:8192],
                  pbuf[:, 0:8192]]
        hq = tb[:, 0:8192]
        hqslot_free = [0, 0]
        dpsum_freeV = [0, 0, 0, 0]
        out_dma = [0, 0]
        y1l_free = [0] * 4
        hq_freeT = 0
        hqT_freeT = 0
        wdsl_freeT = [0, 0, 0, 0]
        wd_lds = [0, 0, 0, 0]
        last_p3b_mm = c["sT"]
        last_p3b_v = c["sV"]
        rtmp = wpre[:, 0:2048]

        for tg in range(4):
            for k in range(4):
                ti = tg * 4 + k
                w.wait("V", "sS", c["sS"])
                w.wait("V", "sV", c["sV"])
                hmv = hmaxp.ap().rearrange("p (o t) -> p o t", t=16)
                w.run("V", lambda e, t2=ti: e.reduce_max(hmax[:], hmv[:, :, t2],
                                                         axis=AXX))
                w.run("V", lambda e: e.drain())
                w.run("V", lambda e: e.tensor_scalar(shtmp[:], hmax[:], 1.0 / 127.0,
                                                     epsv[:], ALU.mult, ALU.add))
                w.run("V", lambda e: e.drain())
                w.run("V", lambda e, t2=ti: e.tensor_scalar(
                    shwd[:, t2:t2 + 1], shtmp[:], wscl[:, 2:3], None, ALU.mult),
                    inc="sV")
                w.wait("S", "sV", c["sV"])
                w.run("S", lambda e: e.activation(sc1[:], shtmp[:], AF.Ln))
                w.run("S", lambda e: e.drain())
                w.run("S", lambda e: e.activation(shinv[:], sc1[:], AF.Exp,
                                                  scale=-1.0))
                w.run("S", lambda e: e.drain())
                nrecip(shinv[:], shtmp[:], 1, inc="sS")
                shinv_done = c["sS"]
                for qt in range(4):
                    qsl = qt % 2
                    hbq = f32a[:, qsl * 2048:(qsl + 1) * 2048]
                    w.wait("Y", "sV", hqslot_free[qsl])
                    if tg == 0 and k == 0 and qt < 2:
                        w.wait("Y", "sT", last_p3b_mm)
                        w.wait("Y", "sV", last_p3b_v)
                        w.wait("Y", "sS", c["sS"])
                        w.wait("Y", "dH0", c["dH0"])
                        w.wait("Y", "dH1", c["dH1"])
                    hld = dma("Y", hbq,
                              h_d.ap()[:, ti * I + qt * 2048:ti * I + (qt + 1) * 2048],
                              "dH%d" % qsl)
                    w.wait("V", "dH%d" % qsl, hld)
                    w.wait("V", "sS", shinv_done)
                    if qt == 0:
                        w.wait("V", "sT", hq_freeT)
                    w.run("V", lambda e: e.drain())
                    w.run("V", lambda e, hb2=hbq: e.tensor_scalar(
                        rtmp, hb2, shinv[:], MAGIC, ALU.mult, ALU.add))
                    w.run("V", lambda e: e.drain())
                    w.run("V", lambda e, q2=qt: e.tensor_scalar(
                        hq[:, q2 * 2048:(q2 + 1) * 2048], rtmp, MAGIC, None,
                        ALU.subtract), inc="sV")
                    hqslot_free[qsl] = c["sV"]
                rounds_done = c["sV"]
                w.wait("T", "sV", rounds_done)
                if tg > 0:
                    w.wait("T", "sT", hqT_freeT)
                for g8 in range(8):
                    transpose_group(hq, g8 * 1024, 8, g8 % 2, hqT_ar[k], g8 * 1024)
                hq_freeT = c["sT"]
            hqT_readyV = c["sV"]
            w.wait("T", "sV", hqT_readyV)
            for og in range(4):
                for k in range(4):
                    ti = tg * 4 + k
                    w.wait("Y", "sV", y1l_free[k])
                    dma("Y", xbuf[:, k * 512:(k + 1) * 512],
                        y_d.ap()[ti * 128:(ti + 1) * 128, og * 512:(og + 1) * 512],
                        "dQ%d" % k)
                y1_all = [c["dQ%d" % k] for k in range(4)]

                def wd_issue(ic2, og2=og, tg2=tg):
                    ssl2 = ic2 % 4
                    eng2 = "S" if ssl2 < 2 else "G"
                    w.wait(eng2, "sT", wdsl_freeT[ssl2])
                    if tg2 == 0 and og2 == 0 and ic2 < 4:
                        w.wait(eng2, "sV", last_quant_V)
                        w.wait(eng2, "sS", last_quant_S)
                        w.wait(eng2, "sT", last_p3b_mm)
                    wd_lds[ssl2] = dma(
                        eng2, csb[:, ssl2 * 512:ssl2 * 512 + 512],
                        wdq_d.ap()[:, ic2 * 2048 + og2 * 512:
                                   ic2 * 2048 + (og2 + 1) * 512],
                        "dD%d" % ssl2)
                for ic2 in range(4):
                    wd_issue(ic2)
                for ic in range(ICN):
                    ssl = ic % 4
                    w.wait("T", "dD%d" % ssl, wd_lds[ssl])
                    if ic == 0:
                        for k in range(4):
                            w.wait("T", "sV", dpsum_freeV[k])
                    for k in range(4):
                        w.run("T", lambda e, k2=k, i2=ic, s2=ssl: e.matmul(
                            downbank[k2][:],
                            hqT_ar[k2][:, i2 * 128:(i2 + 1) * 128],
                            csb[:, s2 * 512:s2 * 512 + 512],
                            start=(i2 == 0), stop=(i2 == ICN - 1)),
                            inc="sT" if k == 3 else None)
                    wdsl_freeT[ssl] = c["sT"]
                    if ic + 4 < ICN:
                        wd_issue(ic + 4)
                mm_done = c["sT"]
                hqT_freeT = mm_done
                for k in range(4):
                    ti = tg * 4 + k
                    osl = k % 2
                    w.wait("V", "sT", mm_done)
                    w.wait("V", "dQ%d" % k, y1_all[k])
                    w.wait("V", "dO%d" % osl, out_dma[osl])
                    w.run("V", lambda e, k2=k, t2=ti, o3=osl:
                          e.scalar_tensor_tensor(
                              xbuf[:, 2048 + o3 * 512:2048 + (o3 + 1) * 512],
                              downbank[k2][:], shwd[:, t2:t2 + 1],
                              xbuf[:, k2 * 512:(k2 + 1) * 512],
                              ALU.mult, ALU.add), inc="sV")
                    dpsum_freeV[k] = c["sV"]
                    y1l_free[k] = c["sV"]
                    w.wait("Y", "sV", c["sV"])
                    out_dma[osl] = dma("Y", y_d.ap()[ti * 128:(ti + 1) * 128,
                                                     og * 512:(og + 1) * 512],
                                       xbuf[:, 2048 + osl * 512:
                                            2048 + (osl + 1) * 512],
                                       "dO%d" % osl)

        for sn in semnames:
            if sn.startswith("d"):
                w.wait("Y", sn, c[sn])

    with nc.Block() as block:
        @block.tensor
        def _(t):
            plan("T", t)

        @block.vector
        def _(v):
            plan("V", v)

        @block.scalar
        def _(s):
            plan("S", s)

        @block.sync
        def _(y):
            plan("Y", y)

        @block.gpsimd
        def _(g):
            plan("G", g)

    nc.finalize()
    return nc


def _host_constants():
    f = np.arange(32, dtype=np.float64)
    freq = 10000.0 ** (-f / 32.0)
    t = np.arange(T, dtype=np.float64)[:, None]
    phase = t * freq[None, :]
    cvals = np.cos(phase).astype(np.float32)
    svals = np.sin(phase).astype(np.float32)
    cs = np.empty((T, 2048), dtype=np.float32)
    cs[:, 0:1024] = np.tile(cvals, (1, 32))
    cs[:, 1024:2048] = np.tile(svals, (1, 32))
    r = np.arange(128)
    mask = np.where(r[None, :] <= r[:, None], 0.0, NEG).astype(np.float32)
    ident = np.eye(128, dtype=ml_dtypes.bfloat16)
    p64 = np.concatenate([np.arange(0, 64, 2), np.arange(1, 64, 2)])
    permq = np.concatenate([h * 64 + p64 for h in range(NH)])
    permk = np.concatenate([h * 64 + p64 for h in range(NKV)])
    return cs, mask, ident, permq, permk


def kernel(L0_attn_x, L0_attn_wq, L0_attn_wk, L0_attn_wv, L0_attn_wo,
           L0_ffn_w_gate, L0_ffn_w_up, L0_ffn_w_down):
    cs, mask, ident, permq, permk = _host_constants()
    x = np.asarray(L0_attn_x, dtype=np.float32)
    common = {
        "wq": np.ascontiguousarray(np.asarray(L0_attn_wq, np.float32)[:, permq]),
        "wk": np.ascontiguousarray(np.asarray(L0_attn_wk, np.float32)[:, permk]),
        "wv": np.ascontiguousarray(np.asarray(L0_attn_wv, np.float32)),
        "wo": np.ascontiguousarray(np.asarray(L0_attn_wo, np.float32)),
        "wgT": np.ascontiguousarray(np.asarray(L0_ffn_w_gate, np.float32).T),
        "wuT": np.ascontiguousarray(np.asarray(L0_ffn_w_up, np.float32).T),
        "wdT": np.ascontiguousarray(np.asarray(L0_ffn_w_down, np.float32).T),
        "cs": cs, "maskadd": mask, "ident": ident,
    }
    if "nc" not in _NC_CACHE:
        _NC_CACHE["nc"] = build_nc()
    nc = _NC_CACHE["nc"]
    in_maps = [dict(common, x=np.ascontiguousarray(x[c])) for c in range(8)]
    res = run_bass_kernel_spmd(nc, in_maps, core_ids=list(range(8)))
    return np.stack([np.asarray(r["y"], np.float32) for r in res.results], axis=0)
